# version 14
# baseline (speedup 1.0000x reference)
import numpy as np
import ml_dtypes
from contextlib import ExitStack

import concourse.bass as bass
import concourse.mybir as mybir
from concourse.bass_utils import run_bass_kernel_spmd

F32 = mybir.dt.float32
BF16 = mybir.dt.bfloat16
I32 = mybir.dt.int32
AF = mybir.ActivationFunctionType
ALU = mybir.AluOpType
AX = mybir.AxisListType

P = 128
D = 1024
DC = 8
NTOK = 2048
NT = 16
B_CORE = 4
N_CORES = 8
EPS_LN = 1e-5
EPS_Q = 1e-30
C_7_5 = float(np.float32(1.0 / 7.5))


def _bc3(ap2d, blocks, width):
    return ap2d.rearrange("p (a b) -> p a b", b=1).broadcast_to([P, blocks, width])


class Sched:

    def __init__(self):
        self.steps = []
        self.writer = {}
        self.readers = {}

    def add(self, eng, fn, reads=(), writes=(), dma=None):
        deps = set()
        for tok in reads:
            w = self.writer.get(tok)
            if w is not None:
                deps.add(w)
        for tok in writes:
            w = self.writer.get(tok)
            if w is not None:
                deps.add(w)
            for r in self.readers.get(tok, ()):
                deps.add(r)
        k = len(self.steps)
        for tok in reads:
            self.readers.setdefault(tok, []).append(k)
        for tok in writes:
            self.writer[tok] = k
            self.readers[tok] = []
        self.steps.append(dict(eng=eng, fn=fn, deps=deps, dma=dma, need=False,
                               mno=None))
        return k


def build_program(debug=False):
    nc = bass.Bass("TRN2", target_bir_lowering=False, debug=False)

    x_d = nc.dram_tensor("x", [B_CORE, NTOK, D], F32, kind="ExternalInput").ap()
    wqkv_d = nc.dram_tensor("wqkv", [D, 3 * D], F32, kind="ExternalInput").ap()
    wvt_d = nc.dram_tensor("wvt", [D, D], F32, kind="ExternalInput").ap()
    owt_d = nc.dram_tensor("owt", [D, D], F32, kind="ExternalInput").ap()
    outb_d = nc.dram_tensor("outb", [P, D], F32, kind="ExternalInput").ap()
    wffn_d = nc.dram_tensor("wffn", [D, D], F32, kind="ExternalInput").ap()
    ffnb_d = nc.dram_tensor("ffnb", [P, D], F32, kind="ExternalInput").ap()
    ident_d = nc.dram_tensor("ident", [P, P], BF16, kind="ExternalInput").ap()
    out_d = nc.dram_tensor("out", [B_CORE, NTOK, D], F32, kind="ExternalOutput").ap()

    with ExitStack() as st:
        def sb(name, shape, dtype):
            return st.enter_context(nc.sbuf_tensor(name, shape, dtype))

        wqkv_s = sb("wqkv_s", [P, DC * 3 * D], BF16)
        wdt_s = sb("wdt_s", [P, DC * D], BF16)
        qt_s = sb("qt_s", [P, DC * NTOK], BF16)
        kt_s = sb("kt_s", [P, DC * NTOK], BF16)
        vo_s = sb("vo_s", [P, NT * D], BF16)
        x_in = sb("x_in", [P, 2 * D], F32)
        outb_s = sb("outb_s", [P, D], BF16)
        ffnb_s = sb("ffnb_s", [P, D], BF16)
        ident_s = sb("ident_s", [P, P], BF16)
        diag_s = sb("diag_s", [P, P], BF16)
        stat_s = sb("stat_s", [P, 64], F32)
        sc_s = sb("sc_s", [P, 96], F32)
        p_sb = sb("p_sb", [P, NTOK], BF16)
        pt_sb = sb("pt_sb", [P, 16 * P], BF16)
        mb3_s = sb("mb3_s", [P, D], BF16)
        x2_s = sb("x2_s", [P, 2 * D], F32)
        qa_s = sb("qa_s", [P, D], F32)
        qi_s = sb("qi_s", [P, D // 2], I32)
        mb1_s = sb("mb1_s", [P, D], BF16)
        mb2_s = sb("mb2_s", [P, D], BF16)
        xd_sb = sb("xd_sb", [P, 2 * D], BF16)
        xdt_sb = sb("xdt_sb", [P, 8 * P], BF16)
        eps_s = sb("eps_s", [P, 1], F32)

        s_ps = [st.enter_context(nc.psum_tensor(f"s_ps{j}", [P, 512], F32))
                for j in range(4)]
        of_ps = [st.enter_context(nc.psum_tensor(f"of_ps{j}", [P, 512], F32))
                 for j in range(2)]
        tr_ps2 = [st.enter_context(nc.psum_tensor(f"tr_ps{j}", [P, 4 * P], F32))
                  for j in range(2)]

        sem = {n: st.enter_context(nc.semaphore("sem_" + n))
               for n in ("t", "v", "a", "g", "ld", "gld", "sto")}

        sc = Sched()
        ld_n = [0]
        gld_n = [0]
        sto_n = [0]

        def sched_ld(eng, dst, src, semname, ctr, reads=(), writes=()):
            ctr[0] += 16
            tgt = ctr[0]

            def fn(e, dst=dst, src=src, sm=sem[semname]):
                e.dma_start(out=dst, in_=src).then_inc(sm, 16)
                return None
            return sc.add(eng, fn, reads=reads, writes=writes,
                          dma=(semname, tgt))

        dbg = {}
        if debug:
            for nm, shape, dt in [
                ("dbg_wq", [P, DC * 3 * D], BF16), ("dbg_wdt", [P, DC * D], BF16),
                ("dbg_qt", [P, DC * NTOK], BF16), ("dbg_kt", [P, DC * NTOK], BF16),
                ("dbg_vo", [P, NT * D], BF16),
            ]:
                dbg[nm] = nc.dram_tensor(nm, shape, dt, kind="ExternalOutput").ap()

        def sched_dump(name, src_ap, reads):
            if not debug or name not in dbg:
                return
            sto_n[0] += 16
            tgt = sto_n[0]

            def fn(e, d=dbg[name], s=src_ap):
                e.dma_start(out=d, in_=s).then_inc(sem["sto"], 16)
                return None
            sc.add("s", fn, reads=reads, dma=("sto", tgt))

        def sched_tr4(src_ap, src_tok, dst_ap, dst_tok, grp, extra_reads=()):
            j = grp % 2
            tok = f"tr{j}"

            def fn_t(e, s=src_ap, j=j):
                r = None
                for kc in range(4):
                    r = e.matmul(out=tr_ps2[j][:, kc * P:(kc + 1) * P],
                                 lhsT=s[:, kc * P:(kc + 1) * P],
                                 rhs=ident_s[:], start=True, stop=True)
                return r
            sc.add("t", fn_t, reads=[src_tok, "ident"] + list(extra_reads),
                   writes=[tok])

            def fn_a(e, d=dst_ap, j=j):
                return e.activation(out=d, in_=tr_ps2[j][:], func=AF.Copy)
            sc.add("a", fn_a, reads=[tok], writes=[dst_tok])

        def _act_preload(e):
            return e.add_instruction(mybir.InstLoadActFuncSet(
                name=nc.get_next_instruction_name(), act_func_set_id=6,
                ins=[], outs=[]))
        sc.add("a", _act_preload)

        nc.const_aps.register(F32, EPS_LN, eps_s[:])

        def _eps_set(e):
            return e.memset(eps_s[:], EPS_LN)
        sc.add("v", _eps_set, writes=["eps"])

        sched_ld("s", ident_s[:], ident_d, "ld", ld_n, writes=["ident"])
        sched_ld("g", outb_s[:], outb_d, "gld", gld_n, writes=["outb"])
        sched_ld("g", ffnb_s[:], ffnb_d, "gld", gld_n, writes=["ffnb"])
        for half in range(2):
            src = wqkv_d[:, half * D:(half + 1) * D].rearrange(
                "(kc p) c -> p kc c", p=P)
            dst = wqkv_s[:].rearrange("p (kc c) -> p kc c", c=3 * D)[
                :, :, half * D:(half + 1) * D]
            sched_ld("g", dst, src, "gld", gld_n, writes=["wqkv_qk"])
        owt_v = qt_s[:, 0:DC * D].rearrange("p (kc c) -> p kc c", c=D)
        sched_ld("g", owt_v, owt_d.rearrange("(kc p) c -> p kc c", p=P),
                 "gld", gld_n, writes=["qt"])

        def _scale_wq(e):
            v = wqkv_s[:].rearrange("p (kc c) -> p kc c", c=3 * D)[:, :, 0:D]
            return e.tensor_scalar(out=v, in0=v, scalar1=1.0 / 32.0,
                                   scalar2=None, op0=ALU.mult)
        sc.add("v", _scale_wq, reads=["wqkv_qk"], writes=["wqkv_qk"])

        for m in range(DC):
            reg = (m % 2) * DC * P
            ktok = f"kt{m % 2}"
            wvt_v = kt_s[:, reg:reg + DC * P].rearrange(
                "p (kc c) -> p kc c", c=P)
            src = wvt_d[:, m * P:(m + 1) * P].rearrange("(kc p) c -> p kc c", p=P)
            sched_ld("g", wvt_v, src, "gld", gld_n, writes=[ktok])

            def _mm_wvo(e, reg=reg):
                r = None
                for nh in range(2):
                    for kc in range(DC):
                        r = e.matmul(
                            out=of_ps[nh][:],
                            lhsT=kt_s[:, reg + kc * P:reg + (kc + 1) * P],
                            rhs=qt_s[:, kc * D + nh * 512: kc * D + nh * 512 + 512],
                            start=(kc == 0), stop=(kc == DC - 1))
                return r
            sc.add("t", _mm_wvo, reads=[ktok, "qt"], writes=["of"])

            def _ev_wvo(e, m=m):
                base = m * 3 * D + 2 * D
                e.activation(out=wqkv_s[:, base:base + 512], in_=of_ps[0][:],
                             func=AF.Copy)
                e.activation(out=wqkv_s[:, base + 512:base + D],
                             in_=of_ps[1][:], func=AF.Copy)
                return None
            sc.add("a", _ev_wvo, reads=["of"], writes=["wqkv_vo"])

        def emit_quantize_head(in_tok, in_ap, rstd_ap=None, rstd_tok=None,
                               heavy="g"):
            rr = [rstd_tok] if rstd_tok else []

            def q_bmax(e, src=in_ap):
                return e.reduce_max(
                    out=sc_s[:, 0:32],
                    in_=src.rearrange("p (a b) -> p a b", b=32),
                    axis=AX.X, apply_absolute_value=True)
            sc.add("v", q_bmax, reads=[in_tok], writes=["sc_b"])

            if rstd_ap is not None:
                def q_bm2(e, ra=rstd_ap):
                    return e.tensor_scalar(out=sc_s[:, 0:32],
                                           in0=sc_s[:, 0:32], scalar1=ra,
                                           scalar2=None, op0=ALU.mult)
                sc.add("v", q_bm2, reads=["sc_b"] + rr, writes=["sc_b"])

            def q_scal(e):
                return e.tensor_scalar(out=sc_s[:, 32:64], in0=sc_s[:, 0:32],
                                       scalar1=C_7_5, scalar2=EPS_Q,
                                       op0=ALU.mult, op1=ALU.max)
            sc.add("v", q_scal, reads=["sc_b"], writes=["sc_s"])

            def q_rsc(e):
                return e.reciprocal(out=sc_s[:, 64:96], in_=sc_s[:, 32:64])
            sc.add("v", q_rsc, reads=["sc_s"], writes=["sc_r"])

            if rstd_ap is not None:
                def q_rs2(e, ra=rstd_ap):
                    return e.tensor_scalar(out=sc_s[:, 64:96],
                                           in0=sc_s[:, 64:96], scalar1=ra,
                                           scalar2=None, op0=ALU.mult)
                sc.add("v", q_rs2, reads=["sc_r"] + rr, writes=["sc_r"])

            def q_u(e, src=in_ap):
                return e.tensor_tensor(
                    out=qa_s[:].rearrange("p (a b) -> p a b", b=32),
                    in0=src.rearrange("p (a b) -> p a b", b=32),
                    in1=_bc3(sc_s[:, 64:96], 32, 32), op=ALU.mult)
            sc.add(heavy, q_u, reads=[in_tok, "sc_r"], writes=["qa"])

        def emit_quantize_tail(out_tok, out_ap, heavy="g"):
            def q_au(e):
                return e.activation(out=mb1_s[:], in_=qa_s[:], func=AF.Abs)
            sc.add("a", q_au, reads=["qa"], writes=["mb1"])

            def q_m1(e):
                return e.tensor_scalar(out=mb2_s[:], in0=mb1_s[:], scalar1=2.0,
                                       scalar2=None, op0=ALU.is_ge)
            sc.add("v", q_m1, reads=["mb1"], writes=["mb2"])

            def q_m2(e):
                return e.tensor_scalar(out=mb1_s[:], in0=mb1_s[:], scalar1=4.0,
                                       scalar2=None, op0=ALU.is_ge)
            sc.add("v", q_m2, reads=["mb1"], writes=["mb1"])

            def q_D(e):
                return e.tensor_scalar(out=mb2_s[:], in0=mb2_s[:], scalar1=-4.0,
                                       scalar2=8.0, op0=ALU.mult, op1=ALU.add)
            sc.add("v", q_D, reads=["mb2"], writes=["mb2"])

            def q_rq(e):
                return e.scalar_tensor_tensor(out=mb3_s[:], in0=mb1_s[:],
                                              scalar=-2.0, in1=mb2_s[:],
                                              op0=ALU.mult, op1=ALU.add)
            sc.add("v", q_rq, reads=["mb1", "mb2"], writes=["mb3"])

            def q_t(e):
                return e.tensor_tensor(out=qa_s[:], in0=qa_s[:], in1=mb3_s[:],
                                       op=ALU.mult)
            sc.add(heavy, q_t, reads=["qa", "mb3"], writes=["qa"])

            def q_r1(e):
                return e.tensor_scalar(out=qi_s[:], in0=qa_s[:, 0:512],
                                       scalar1=0.0, scalar2=None, op0=ALU.add)
            sc.add("v", q_r1, reads=["qa"], writes=["qi"])

            def q_rf1(e):
                return e.tensor_scalar(out=qa_s[:, 0:512], in0=qi_s[:],
                                       scalar1=0, scalar2=None, op0=ALU.add)
            sc.add("v", q_rf1, reads=["qi"], writes=["qa"])

            def q_r2(e):
                return e.tensor_scalar(out=qi_s[:], in0=qa_s[:, 512:D],
                                       scalar1=0.0, scalar2=None, op0=ALU.add)
            sc.add("v", q_r2, reads=["qa"], writes=["qi"])

            def q_T2(e):
                return e.tensor_scalar(out=mb2_s[:], in0=mb2_s[:],
                                       scalar1=-1.0 / 32.0, scalar2=0.375,
                                       op0=ALU.mult, op1=ALU.add)
            sc.add("v", q_T2, reads=["mb2"], writes=["mb2"])

            def q_qtm(e):
                return e.scalar_tensor_tensor(out=mb2_s[:], in0=mb1_s[:],
                                              scalar=0.25, in1=mb2_s[:],
                                              op0=ALU.mult, op1=ALU.add)
            sc.add("v", q_qtm, reads=["mb1", "mb2"], writes=["mb2"])

            def q_rf2(e):
                return e.tensor_scalar(out=qa_s[:, 512:D], in0=qi_s[:],
                                       scalar1=0, scalar2=None, op0=ALU.add)
            sc.add("v", q_rf2, reads=["qi"], writes=["qa"])

            def q_w1(e):
                return e.tensor_tensor(
                    out=qa_s[:].rearrange("p (a b) -> p a b", b=32),
                    in0=qa_s[:].rearrange("p (a b) -> p a b", b=32),
                    in1=_bc3(sc_s[:, 32:64], 32, 32), op=ALU.mult)
            sc.add(heavy, q_w1, reads=["qa", "sc_s"], writes=["qa"])

            def q_xd(e, dst=out_ap):
                return e.tensor_tensor(out=dst, in0=qa_s[:], in1=mb2_s[:],
                                       op=ALU.mult)
            sc.add(heavy, q_xd, reads=["qa", "mb2"], writes=[out_tok])

        def emit_wd_chunk(mc):
            sched_ld("s", x2_s[:, 0:D], wffn_d[mc * P:(mc + 1) * P, :],
                     "ld", ld_n, writes=["x2_0"])
            emit_quantize_head("x2_0", x2_s[:, 0:D], heavy="v")
            emit_quantize_tail("xd_0", xd_sb[:, 0:D], heavy="v")
            for g in range(2):
                dst = wdt_s[:].rearrange("p (kc c) -> p kc c", c=D)[
                    :, 4 * g:4 * g + 4, mc * P:(mc + 1) * P]
                sched_tr4(xd_sb[:, g * 512:(g + 1) * 512], "xd_0",
                          dst, "wdt", g)

        sched_dump("dbg_wq", wqkv_s[:], ["wqkv_qk", "wqkv_vo"])
        sched_dump("dbg_wdt", wdt_s[:], ["wdt"])

        for b in range(B_CORE):
            def a_ln(t, b=b):
                xp = t % 2
                xtok = f"x_{xp}"
                sched_ld("s", x_in[:, xp * D:(xp + 1) * D],
                         x_d[b, t * P:(t + 1) * P, :], "ld", ld_n,
                         writes=[xtok])
                sp = 32 * (t % 2)
                stm = f"stm_{t % 2}"
                stv = f"stv_{t % 2}"

                def a_sum(e, xp=xp, sp=sp):
                    return e.reduce_sum(out=stat_s[:, sp:sp + 1],
                                        in_=x_in[:, xp * D:(xp + 1) * D],
                                        axis=AX.X)
                sc.add("v", a_sum, reads=[xtok], writes=[stm])

                def a_m(e, sp=sp):
                    return e.tensor_scalar(out=stat_s[:, sp + 1:sp + 2],
                                           in0=stat_s[:, sp:sp + 1],
                                           scalar1=1.0 / D, scalar2=None,
                                           op0=ALU.mult)
                sc.add("v", a_m, reads=[stm], writes=[stm])

                def a_xc(e, xp=xp, sp=sp):
                    return e.tensor_scalar(out=mb2_s[:],
                                           in0=x_in[:, xp * D:(xp + 1) * D],
                                           scalar1=stat_s[:, sp + 1:sp + 2],
                                           scalar2=None, op0=ALU.subtract)
                sc.add("v", a_xc, reads=[xtok, stm], writes=["mb2"])

                def a_sq(e, sp=sp):
                    return e.scalar_tensor_tensor(
                        out=mb1_s[:], in0=mb2_s[:], scalar=1.0, in1=mb2_s[:],
                        op0=ALU.mult, op1=ALU.mult,
                        accum_out=stat_s[:, sp + 2:sp + 3])
                sc.add("v", a_sq, reads=["mb2"], writes=["mb1", stv])

                def a_rstd(e, sp=sp):
                    e.activation(out=stat_s[:, sp + 3:sp + 4],
                                 in_=stat_s[:, sp + 2:sp + 3], func=AF.Ln,
                                 scale=1.0 / D, bias=EPS_LN)
                    e.drain()
                    return e.activation(out=stat_s[:, sp + 4:sp + 5],
                                        in_=stat_s[:, sp + 3:sp + 4],
                                        func=AF.Exp, scale=-0.5)
                sc.add("a", a_rstd, reads=[stv, "eps"], writes=[stv])

                def a_diag(e, sp=sp):
                    return e.tensor_scalar(out=diag_s[:], in0=ident_s[:],
                                           scalar1=stat_s[:, sp + 4:sp + 5],
                                           scalar2=None, op0=ALU.mult)
                sc.add("v", a_diag, reads=["ident", stv], writes=["diag"])

            def a_pe(t, b=b):
                xp = t % 2
                xntok = f"xnt_{t % 2}"
                xnt_view = p_sb[:, xp * D:(xp + 1) * D]
                for g in range(2):
                    j = g % 2
                    tok = f"tr{j}"

                    def a_trx(e, g=g, j=j):
                        r = None
                        for kc in range(4):
                            r = e.matmul(
                                out=tr_ps2[j][:, kc * P:(kc + 1) * P],
                                lhsT=mb2_s[:, (g * 4 + kc) * P:
                                            (g * 4 + kc + 1) * P],
                                rhs=diag_s[:], start=True, stop=True)
                        return r
                    sc.add("t", a_trx, reads=["mb2", "diag"], writes=[tok])

                    def a_evx(e, dst=xnt_view, g=g, j=j):
                        return e.activation(out=dst[:, g * 512:(g + 1) * 512],
                                            in_=tr_ps2[j][:], func=AF.Copy)
                    sc.add("a", a_evx, reads=[tok], writes=[xntok])

            def a_mm(t, b=b):
                xntok = f"xnt_{t % 2}"
                xnt_view = p_sb[:, (t % 2) * D:(t % 2 + 1) * D]
                for ocg in range(4):
                    def a_qk(e, ocg=ocg, src=xnt_view):
                        r = None
                        for j in range(4):
                            oc = ocg * 4 + j
                            for kc in range(DC):
                                r = e.matmul(
                                    out=s_ps[ocg][:, j * P:(j + 1) * P],
                                    lhsT=wqkv_s[:, kc * 3 * D + oc * P:
                                                kc * 3 * D + oc * P + P],
                                    rhs=src[:, kc * P:(kc + 1) * P],
                                    start=(kc == 0), stop=(kc == DC - 1))
                        return r
                    sc.add("t", a_qk, reads=[xntok, "wqkv_qk"],
                           writes=[f"s{ocg}"])

                    def a_evqk(e, ocg=ocg, t=t):
                        dstbase = qt_s if ocg < 2 else kt_s
                        oc0 = (ocg * 4) % 8
                        dst = dstbase[:].rearrange("p (a c) -> p a c", c=NTOK)[
                            :, oc0:oc0 + 4, t * P:(t + 1) * P]
                        return e.activation(
                            out=dst,
                            in_=s_ps[ocg][:].rearrange("p (a c) -> p a c", c=P),
                            func=AF.Copy)
                    sc.add("a", a_evqk, reads=[f"s{ocg}"],
                           writes=(["qt"] if ocg < 2 else
                                   ["kt", "kt0", "kt1"]))

                def a_vo(e, src=xnt_view):
                    r = None
                    for nh in range(2):
                        for kc in range(DC):
                            r = e.matmul(
                                out=of_ps[nh][:],
                                lhsT=src[:, kc * P:(kc + 1) * P],
                                rhs=wqkv_s[:, kc * 3 * D + 2 * D + nh * 512:
                                           kc * 3 * D + 2 * D + nh * 512 + 512],
                                start=(kc == 0), stop=(kc == DC - 1))
                    return r
                sc.add("t", a_vo, reads=[xntok, "wqkv_vo"], writes=["of"])

                def a_evvo(e, t=t):
                    e.activation(out=vo_s[:, t * D:t * D + 512],
                                 in_=of_ps[0][:], func=AF.Copy)
                    e.activation(out=vo_s[:, t * D + 512:(t + 1) * D],
                                 in_=of_ps[1][:], func=AF.Copy)
                    return None
                sc.add("a", a_evvo, reads=["of"], writes=["vo"])

            a_ln(0)
            for t in range(NT):
                a_pe(t)
                if b == 0 and t % 2 == 0:
                    emit_wd_chunk(t // 2)
                if t + 1 < NT:
                    a_ln(t + 1)
                a_mm(t)

            if b == 0:
                sched_dump("dbg_qt", qt_s[:], ["qt"])
                sched_dump("dbg_kt", kt_s[:], ["kt"])
                sched_dump("dbg_vo", vo_s[:], ["vo"])

            def stage_S1(i, b=b):
                xp = i % 2
                sp = 32 * (i % 2)
                stx = f"stx_{i % 2}"
                sched_ld("s", x_in[:, xp * D:(xp + 1) * D],
                         x_d[b, i * P:(i + 1) * P, :], "ld", ld_n,
                         writes=[f"x_{xp}"])

                def b_xpb(e, xp=xp):
                    v = x_in[:, xp * D:(xp + 1) * D]
                    return e.tensor_tensor(out=v, in0=v, in1=outb_s[:],
                                           op=ALU.add)
                sc.add("g", b_xpb, reads=[f"x_{xp}", "outb"],
                       writes=[f"x_{xp}"])

                for nb in range(4):
                    def b_s(e, i=i, nb=nb):
                        r = None
                        for kc in range(DC):
                            r = e.matmul(
                                out=s_ps[nb][:],
                                lhsT=qt_s[:, kc * NTOK + i * P:
                                          kc * NTOK + i * P + P],
                                rhs=kt_s[:, kc * NTOK + nb * 512:
                                         kc * NTOK + nb * 512 + 512],
                                start=(kc == 0), stop=(kc == DC - 1))
                        return r
                    sc.add("t", b_s, reads=["qt", "kt"], writes=[f"s{nb}"])

                def b_max(e, sp=sp):
                    for nb in range(4):
                        e.reduce_max(out=stat_s[:, sp + 8 + nb:sp + 9 + nb],
                                     in_=s_ps[nb][:], axis=AX.X, negate=True)
                    e.drain()
                    return e.tensor_reduce(out=stat_s[:, sp + 12:sp + 13],
                                           in_=stat_s[:, sp + 8:sp + 12],
                                           axis=AX.X, op=ALU.min)
                sc.add("v", b_max, reads=["s0", "s1", "s2", "s3"],
                       writes=[stx])

            def stage_S2(i, b=b):
                sp = 32 * (i % 2)
                stx = f"stx_{i % 2}"
                std = f"std_{i % 2}"

                def b_exp(e, sp=sp):
                    for nb in range(4):
                        e.activation(out=p_sb[:, nb * 512:(nb + 1) * 512],
                                     in_=s_ps[nb][:], func=AF.Exp,
                                     bias=stat_s[:, sp + 12:sp + 13],
                                     scale=1.0,
                                     accum_out=stat_s[:, sp + 16 + nb:
                                                      sp + 17 + nb])
                    return None
                sc.add("a", b_exp,
                       reads=["s0", "s1", "s2", "s3", stx],
                       writes=["p", std])

                def b_den(e, sp=sp):
                    e.reduce_sum(out=stat_s[:, sp + 20:sp + 21],
                                 in_=stat_s[:, sp + 16:sp + 20], axis=AX.X)
                    e.drain()
                    return e.reciprocal(out=stat_s[:, sp + 21:sp + 22],
                                        in_=stat_s[:, sp + 20:sp + 21])
                sc.add("v", b_den, reads=[std], writes=[std])

            def stage_trp(i, q, b=b):
                sched_tr4(p_sb[:, (q * 4) * P:(q * 4 + 4) * P], "p",
                          pt_sb[:, q * 512:(q + 1) * 512], f"ptr{q}", q)

            def stage_o(i, q, b=b):
                def b_o(e, q=q):
                    rr = None
                    for j in range(4):
                        kb = q * 4 + j
                        for nh in range(2):
                            rr = e.matmul(
                                out=of_ps[nh][:],
                                lhsT=pt_sb[:, kb * P:(kb + 1) * P],
                                rhs=vo_s[:, kb * D + nh * 512:
                                         kb * D + nh * 512 + 512],
                                start=(kb == 0), stop=(kb == NT - 1))
                    return rr
                sc.add("t", b_o, reads=[f"ptr{q}", "vo"], writes=["of"])

            def stage_Oep(i, b=b):
                xp = i % 2
                sp = 32 * (i % 2)
                stm = f"stm_{i % 2}"
                stv = f"stv_{i % 2}"
                std = f"std_{i % 2}"
                x2tok = f"x2_{i % 2}"
                x2v = x2_s[:, (i % 2) * D:(i % 2 + 1) * D]

                def b_oep(e, xp=xp, sp=sp, x2v=x2v):
                    for nh in range(2):
                        e.scalar_tensor_tensor(
                            out=x2v[:, nh * 512:(nh + 1) * 512],
                            in0=of_ps[nh][:],
                            scalar=stat_s[:, sp + 21:sp + 22],
                            in1=x_in[:, xp * D + nh * 512:
                                     xp * D + nh * 512 + 512],
                            op0=ALU.mult, op1=ALU.add)
                    return None
                sc.add("v", b_oep, reads=["of", std, f"x_{xp}"],
                       writes=[x2tok])

                def b_nsum(e, x2v=x2v, sp=sp):
                    return e.reduce_sum(out=stat_s[:, sp + 0:sp + 1], in_=x2v,
                                        axis=AX.X)
                sc.add("v", b_nsum, reads=[x2tok], writes=[stm])

                def b_sq2(e, x2v=x2v, sp=sp):
                    return e.activation(out=mb1_s[:], in_=x2v, func=AF.Square,
                                        accum_out=stat_s[:, sp + 2:sp + 3])
                sc.add("a", b_sq2, reads=[x2tok], writes=["mb1", stv])

                def b_negm(e, sp=sp):
                    return e.tensor_scalar(out=stat_s[:, sp + 1:sp + 2],
                                           in0=stat_s[:, sp + 0:sp + 1],
                                           scalar1=1.0 / D, scalar2=None,
                                           op0=ALU.mult)
                sc.add("v", b_negm, reads=[stm], writes=[stm])

                def b_xc2(e, x2v=x2v, sp=sp):
                    return e.tensor_scalar(out=qa_s[:], in0=x2v,
                                           scalar1=stat_s[:, sp + 1:sp + 2],
                                           scalar2=None, op0=ALU.subtract)
                sc.add("v", b_xc2, reads=[x2tok, stm], writes=["qa"])

                def b_var(e, sp=sp):
                    e.tensor_scalar(out=stat_s[:, sp + 6:sp + 7],
                                    in0=stat_s[:, sp + 1:sp + 2],
                                    scalar1=stat_s[:, sp + 1:sp + 2],
                                    scalar2=None, op0=ALU.mult)
                    e.drain()
                    return e.tensor_scalar(out=stat_s[:, sp + 5:sp + 6],
                                           in0=stat_s[:, sp + 2:sp + 3],
                                           scalar1=1.0 / D,
                                           scalar2=stat_s[:, sp + 6:sp + 7],
                                           op0=ALU.mult, op1=ALU.subtract)
                sc.add("v", b_var, reads=[stv, stm], writes=[stv])

                def b_rstd(e, sp=sp):
                    e.activation(out=stat_s[:, sp + 3:sp + 4],
                                 in_=stat_s[:, sp + 5:sp + 6], func=AF.Ln,
                                 bias=EPS_LN)
                    e.drain()
                    return e.activation(out=stat_s[:, sp + 4:sp + 5],
                                        in_=stat_s[:, sp + 3:sp + 4],
                                        func=AF.Exp, scale=-0.5)
                sc.add("a", b_rstd, reads=[stv, "eps"], writes=[stv])

                emit_quantize_head("qa", qa_s[:],
                                   rstd_ap=stat_s[:, sp + 4:sp + 5],
                                   rstd_tok=stv)

            def stage_QT(i, b=b):
                emit_quantize_tail(f"xd_{i % 2}",
                                   xd_sb[:, (i % 2) * D:(i % 2 + 1) * D])

            def stage_trxd(i, b=b):
                xdv = xd_sb[:, (i % 2) * D:(i % 2 + 1) * D]
                for g in range(2):
                    sched_tr4(xdv[:, g * 512:(g + 1) * 512], f"xd_{i % 2}",
                              xdt_sb[:, g * 512:(g + 1) * 512], "xdt", g)

            def stage_Fb(i, b=b):
                x2tok = f"x2_{i % 2}"
                x2v = x2_s[:, (i % 2) * D:(i % 2 + 1) * D]

                def b_addfb(e, x2v=x2v):
                    return e.tensor_tensor(out=x2v, in0=x2v, in1=ffnb_s[:],
                                           op=ALU.add)
                sc.add("g", b_addfb, reads=[x2tok, "ffnb"], writes=[x2tok])

            def stage_F(i, b=b):
                x2tok = f"x2_{i % 2}"
                x2v = x2_s[:, (i % 2) * D:(i % 2 + 1) * D]

                def b_f(e):
                    r = None
                    for nh in range(2):
                        for kc in range(DC):
                            r = e.matmul(
                                out=of_ps[nh][:],
                                lhsT=xdt_sb[:, kc * P:(kc + 1) * P],
                                rhs=wdt_s[:, kc * D + nh * 512:
                                          kc * D + nh * 512 + 512],
                                start=(kc == 0), stop=(kc == DC - 1))
                    return r
                sc.add("t", b_f, reads=["xdt", "wdt"], writes=["of"])

                def b_fep(e, x2v=x2v):
                    e.tensor_tensor(out=x2v[:, 0:512], in0=x2v[:, 0:512],
                                    in1=of_ps[0][:], op=ALU.add)
                    e.tensor_tensor(out=x2v[:, 512:D], in0=x2v[:, 512:D],
                                    in1=of_ps[1][:], op=ALU.add)
                    return None
                sc.add("v", b_fep, reads=["of", x2tok], writes=[x2tok])

                sto_n[0] += 16
                tgt = sto_n[0]

                def b_sto(e, b=b, i=i, x2v=x2v):
                    e.dma_start(out=out_d[b, i * P:(i + 1) * P, :],
                                in_=x2v).then_inc(sem["sto"], 16)
                    return None
                sc.add("s", b_sto, reads=[x2tok], dma=("sto", tgt))

            for i in range(NT + 3):
                if i < NT:
                    stage_S1(i)
                if i - 3 >= 0:
                    stage_trxd(i - 3)
                    stage_Fb(i - 3)
                if 0 <= i - 1 < NT:
                    for q in range(4):
                        stage_trp(i - 1, q)
                if i - 3 >= 0:
                    stage_F(i - 3)
                if 0 <= i - 2 < NT:
                    stage_QT(i - 2)
                if i < NT:
                    stage_S2(i)
                if 0 <= i - 1 < NT:
                    for q in range(4):
                        stage_o(i - 1, q)
                if 0 <= i - 1 < NT:
                    stage_Oep(i - 1)

        total_sto = sto_n[0]

        def _fin(e):
            e.wait_ge(sem["sto"], total_sto)
            return None
        sc.add("s", _fin)

        steps = sc.steps
        for s_ in steps:
            for d in s_["deps"]:
                steps[d]["need"] = True
        mcount = {}
        for s_ in steps:
            if s_["dma"] is None and s_["need"]:
                en = s_["eng"]
                mcount[en] = mcount.get(en, 0) + 1
                s_["mno"] = mcount[en]

        with nc.Block() as block:
            def runner(name):
                def run(eng):
                    covered = {}
                    self_cov = 0
                    for k, s_ in enumerate(steps):
                        if s_["eng"] != name:
                            continue
                        waits = {}
                        for d in sorted(s_["deps"]):
                            ds = steps[d]
                            if ds["dma"] is not None:
                                smn, tgt = ds["dma"]
                                if covered.get(smn, 0) < tgt:
                                    waits[smn] = max(waits.get(smn, 0), tgt)
                            elif ds["eng"] == name:
                                assert ds["mno"] is not None, (k, d)
                                if ds["mno"] > self_cov:
                                    waits[name] = max(waits.get(name, 0),
                                                      ds["mno"])
                            else:
                                assert ds["mno"] is not None, (k, d, ds)
                                if covered.get(ds["eng"], 0) < ds["mno"]:
                                    waits[ds["eng"]] = max(
                                        waits.get(ds["eng"], 0), ds["mno"])
                        for smn, v in waits.items():
                            eng.wait_ge(sem[smn], v)
                            if smn == name:
                                self_cov = max(self_cov, v)
                            else:
                                covered[smn] = max(covered.get(smn, 0), v)
                        r = s_["fn"](eng)
                        if s_["dma"] is None and s_["mno"] is not None:
                            if r is None:
                                eng.drain().then_inc(sem[name], 1)
                            else:
                                r.then_inc(sem[name], 1)
                return run
            block.vector(runner("v"))
            block.scalar(runner("a"))
            block.tensor(runner("t"))
            block.gpsimd(runner("g"))
            block.sync(runner("s"))

    return nc


_NC_CACHE = None


def kernel(L0_attn_x, L0_attn_weight_qkv, L0_attn_out_w, L0_attn_out_b,
           L0_ffn_weight, L0_ffn_bias):
    global _NC_CACHE
    x = np.ascontiguousarray(np.asarray(L0_attn_x, dtype=np.float32))
    wqkv = np.ascontiguousarray(np.asarray(L0_attn_weight_qkv, dtype=np.float32))
    out_w = np.asarray(L0_attn_out_w, dtype=np.float32)
    out_b = np.asarray(L0_attn_out_b, dtype=np.float32)
    wffn = np.ascontiguousarray(np.asarray(L0_ffn_weight, dtype=np.float32))
    ffnb = np.asarray(L0_ffn_bias, dtype=np.float32)

    wvt = np.ascontiguousarray(wqkv[:, 2 * D:3 * D].T)
    owt = np.ascontiguousarray(out_w.T)
    outb_bc = np.ascontiguousarray(np.broadcast_to(out_b, (P, D)))
    ffnb_bc = np.ascontiguousarray(np.broadcast_to(ffnb, (P, D)))
    ident = np.eye(P, dtype=ml_dtypes.bfloat16)

    if _NC_CACHE is None:
        _NC_CACHE = build_program()
    nc = _NC_CACHE

    in_maps = []
    for c in range(N_CORES):
        in_maps.append({
            "x": np.ascontiguousarray(x[c * B_CORE:(c + 1) * B_CORE]),
            "wqkv": wqkv,
            "wvt": wvt,
            "owt": owt,
            "outb": outb_bc,
            "wffn": wffn,
            "ffnb": ffnb_bc,
            "ident": ident,
        })
    res = run_bass_kernel_spmd(nc, in_maps, core_ids=list(range(N_CORES)))
    out = np.concatenate([res.results[c]["out"] for c in range(N_CORES)], axis=0)
    return out.astype(np.float32)


# version 28
# speedup vs baseline: 1.0610x; 1.0610x over previous
import numpy as np
import ml_dtypes
from contextlib import ExitStack

import concourse.bass as bass
import concourse.mybir as mybir
from concourse.bass_utils import run_bass_kernel_spmd

F32 = mybir.dt.float32
BF16 = mybir.dt.bfloat16
I32 = mybir.dt.int32
AF = mybir.ActivationFunctionType
ALU = mybir.AluOpType
AX = mybir.AxisListType

P = 128
D = 1024
DC = 8
NTOK = 2048
NT = 16
B_CORE = 4
N_CORES = 8
EPS_LN = 1e-5
EPS_Q = 1e-30
C_7_5 = float(np.float32(1.0 / 7.5))


def _bc3(ap2d, blocks, width):
    return ap2d.rearrange("p (a b) -> p a b", b=1).broadcast_to([P, blocks, width])


class Sched:

    def __init__(self):
        self.steps = []
        self.writer = {}
        self.readers = {}

    def add(self, eng, fn, reads=(), writes=(), dma=None):
        deps = set()
        for tok in reads:
            w = self.writer.get(tok)
            if w is not None:
                deps.add(w)
        for tok in writes:
            w = self.writer.get(tok)
            if w is not None:
                deps.add(w)
            for r in self.readers.get(tok, ()):
                deps.add(r)
        k = len(self.steps)
        for tok in reads:
            self.readers.setdefault(tok, []).append(k)
        for tok in writes:
            self.writer[tok] = k
            self.readers[tok] = []
        self.steps.append(dict(eng=eng, fn=fn, deps=deps, dma=dma, need=False,
                               mno=None))
        return k


def build_program(debug=False):
    nc = bass.Bass("TRN2", target_bir_lowering=False, debug=False)

    x_d = nc.dram_tensor("x", [B_CORE, NTOK, D], F32, kind="ExternalInput").ap()
    wqkv_d = nc.dram_tensor("wqkv", [D, 3 * D], F32, kind="ExternalInput").ap()
    wvt_d = nc.dram_tensor("wvt", [D, D], F32, kind="ExternalInput").ap()
    owt_d = nc.dram_tensor("owt", [D, D], F32, kind="ExternalInput").ap()
    outb_d = nc.dram_tensor("outb", [P, D], F32, kind="ExternalInput").ap()
    wffn_d = nc.dram_tensor("wffn", [D, D], F32, kind="ExternalInput").ap()
    ffnb_d = nc.dram_tensor("ffnb", [P, D], F32, kind="ExternalInput").ap()
    ident_d = nc.dram_tensor("ident", [P, P], BF16, kind="ExternalInput").ap()
    out_d = nc.dram_tensor("out", [B_CORE, NTOK, D], F32, kind="ExternalOutput").ap()

    with ExitStack() as st:
        def sb(name, shape, dtype):
            return st.enter_context(nc.sbuf_tensor(name, shape, dtype))

        wqkv_s = sb("wqkv_s", [P, DC * 3 * D], BF16)
        wdt_s = sb("wdt_s", [P, DC * D], BF16)
        qt_s = sb("qt_s", [P, DC * NTOK], BF16)
        kt_s = sb("kt_s", [P, DC * NTOK], BF16)
        vo_s = sb("vo_s", [P, NT * D], BF16)
        x_in = sb("x_in", [P, 2 * D], F32)
        outb_s = sb("outb_s", [P, D], BF16)
        ffnb_s = sb("ffnb_s", [P, D], BF16)
        ident_s = sb("ident_s", [P, P], BF16)
        diag_s = sb("diag_s", [P, P], BF16)
        diagb_s = sb("diagb_s", [P, P], BF16)
        stat_s = sb("stat_s", [P, 64], F32)
        sc_s = sb("sc_s", [P, 96], F32)
        p_sb = sb("p_sb", [P, NTOK], BF16)
        pt_sb = sb("pt_sb", [P, 16 * P], BF16)
        mb3_s = sb("mb3_s", [P, D], BF16)
        x2_s = sb("x2_s", [P, 2 * D], F32)
        qa_s = sb("qa_s", [P, D], F32)
        qi_s = sb("qi_s", [P, D // 2], I32)
        mb1_s = sb("mb1_s", [P, D], BF16)
        mb2_s = sb("mb2_s", [P, D], BF16)
        xd_sb = sb("xd_sb", [P, 2 * D], BF16)
        xdt_sb = sb("xdt_sb", [P, 8 * P], BF16)
        eps_s = sb("eps_s", [P, 1], F32)

        s_ps = [st.enter_context(nc.psum_tensor(f"s_ps{j}", [P, 512], F32))
                for j in range(4)]
        of_ps = [st.enter_context(nc.psum_tensor(f"of_ps{j}", [P, 512], F32))
                 for j in range(2)]
        tr_ps2 = [st.enter_context(nc.psum_tensor(f"tr_ps{j}", [P, 4 * P], F32))
                  for j in range(2)]

        sem = {n: st.enter_context(nc.semaphore("sem_" + n))
               for n in ("t", "v", "a", "g", "ld", "gld", "sto")}

        sc = Sched()
        ld_n = [0]
        gld_n = [0]
        sto_n = [0]

        def sched_ld(eng, dst, src, semname, ctr, reads=(), writes=()):
            ctr[0] += 16
            tgt = ctr[0]

            def fn(e, dst=dst, src=src, sm=sem[semname]):
                e.dma_start(out=dst, in_=src).then_inc(sm, 16)
                return None
            return sc.add(eng, fn, reads=reads, writes=writes,
                          dma=(semname, tgt))

        dbg = {}
        if debug:
            for nm, shape, dt in [
                ("dbg_wq", [P, DC * 3 * D], BF16), ("dbg_wdt", [P, DC * D], BF16),
                ("dbg_qt", [P, DC * NTOK], BF16), ("dbg_kt", [P, DC * NTOK], BF16),
                ("dbg_vo", [P, NT * D], BF16),
            ]:
                dbg[nm] = nc.dram_tensor(nm, shape, dt, kind="ExternalOutput").ap()

        def sched_dump(name, src_ap, reads):
            if not debug or name not in dbg:
                return
            sto_n[0] += 16
            tgt = sto_n[0]

            def fn(e, d=dbg[name], s=src_ap):
                e.dma_start(out=d, in_=s).then_inc(sem["sto"], 16)
                return None
            sc.add("s", fn, reads=reads, dma=("sto", tgt))

        def sched_tr4(src_ap, src_tok, dst_ap, dst_tok, grp, extra_reads=()):
            j = grp % 2
            tok = f"tr{j}"

            def fn_t(e, s=src_ap, j=j):
                r = None
                for kc in range(4):
                    r = e.matmul(out=tr_ps2[j][:, kc * P:(kc + 1) * P],
                                 lhsT=s[:, kc * P:(kc + 1) * P],
                                 rhs=ident_s[:], start=True, stop=True)
                return r
            sc.add("t", fn_t, reads=[src_tok, "ident"] + list(extra_reads),
                   writes=[tok])

            def fn_a(e, d=dst_ap, j=j):
                return e.activation(out=d, in_=tr_ps2[j][:], func=AF.Copy)
            sc.add("a", fn_a, reads=[tok], writes=[dst_tok])

        def _act_preload(e):
            return e.add_instruction(mybir.InstLoadActFuncSet(
                name=nc.get_next_instruction_name(), act_func_set_id=6,
                ins=[], outs=[]))
        sc.add("a", _act_preload)

        nc.const_aps.register(F32, EPS_LN, eps_s[:])

        def _eps_set(e):
            return e.memset(eps_s[:], EPS_LN)
        sc.add("v", _eps_set, writes=["eps"])

        sched_ld("s", ident_s[:], ident_d, "ld", ld_n, writes=["ident"])
        sched_ld("g", outb_s[:], outb_d, "gld", gld_n, writes=["outb"])
        sched_ld("g", ffnb_s[:], ffnb_d, "gld", gld_n, writes=["ffnb"])
        for half in range(2):
            src = wqkv_d[:, half * D:(half + 1) * D].rearrange(
                "(kc p) c -> p kc c", p=P)
            dst = wqkv_s[:].rearrange("p (kc c) -> p kc c", c=3 * D)[
                :, :, half * D:(half + 1) * D]
            sched_ld("g", dst, src, "gld", gld_n, writes=["wqkv_qk"])
        owt_v = qt_s[:, 0:DC * D].rearrange("p (kc c) -> p kc c", c=D)
        sched_ld("g", owt_v, owt_d.rearrange("(kc p) c -> p kc c", p=P),
                 "gld", gld_n, writes=["qt"])

        def _scale_wq(e):
            v = wqkv_s[:].rearrange("p (kc c) -> p kc c", c=3 * D)[:, :, 0:D]
            return e.tensor_scalar(out=v, in0=v, scalar1=1.0 / 32.0,
                                   scalar2=None, op0=ALU.mult)
        sc.add("v", _scale_wq, reads=["wqkv_qk"], writes=["wqkv_qk"])

        for m in range(DC):
            reg = (m % 2) * DC * P
            ktok = f"kt{m % 2}"
            wvt_v = kt_s[:, reg:reg + DC * P].rearrange(
                "p (kc c) -> p kc c", c=P)
            src = wvt_d[:, m * P:(m + 1) * P].rearrange("(kc p) c -> p kc c", p=P)
            sched_ld("g", wvt_v, src, "gld", gld_n, writes=[ktok])

            def _mm_wvo(e, reg=reg):
                r = None
                for nh in range(2):
                    for kc in range(DC):
                        r = e.matmul(
                            out=of_ps[nh][:],
                            lhsT=kt_s[:, reg + kc * P:reg + (kc + 1) * P],
                            rhs=qt_s[:, kc * D + nh * 512: kc * D + nh * 512 + 512],
                            start=(kc == 0), stop=(kc == DC - 1))
                return r
            sc.add("t", _mm_wvo, reads=[ktok, "qt"], writes=["of0", "of1"])

            def _ev_wvo(e, m=m):
                base = m * 3 * D + 2 * D
                e.activation(out=wqkv_s[:, base:base + 512], in_=of_ps[0][:],
                             func=AF.Copy)
                e.activation(out=wqkv_s[:, base + 512:base + D],
                             in_=of_ps[1][:], func=AF.Copy)
                return None
            sc.add("a", _ev_wvo, reads=["of0", "of1"], writes=["wqkv_vo"])

        def emit_quantize_head(in_tok, in_ap, rstd_ap=None, rstd_tok=None,
                               heavy="g"):
            rr = [rstd_tok] if rstd_tok else []

            def q_bmax(e, src=in_ap):
                return e.reduce_max(
                    out=sc_s[:, 0:32],
                    in_=src.rearrange("p (a b) -> p a b", b=32),
                    axis=AX.X, apply_absolute_value=True)
            sc.add("v", q_bmax, reads=[in_tok], writes=["sc_b"])

            if rstd_ap is not None:
                def q_bm2(e, ra=rstd_ap):
                    return e.tensor_scalar(out=sc_s[:, 0:32],
                                           in0=sc_s[:, 0:32], scalar1=ra,
                                           scalar2=None, op0=ALU.mult)
                sc.add("v", q_bm2, reads=["sc_b"] + rr, writes=["sc_b"])

            def q_scal(e):
                return e.tensor_scalar(out=sc_s[:, 32:64], in0=sc_s[:, 0:32],
                                       scalar1=C_7_5, scalar2=EPS_Q,
                                       op0=ALU.mult, op1=ALU.max)
            sc.add("v", q_scal, reads=["sc_b"], writes=["sc_s"])

            def q_rsc(e):
                return e.reciprocal(out=sc_s[:, 64:96], in_=sc_s[:, 32:64])
            sc.add("v", q_rsc, reads=["sc_s"], writes=["sc_r"])

            if rstd_ap is not None:
                def q_rs2(e, ra=rstd_ap):
                    return e.tensor_scalar(out=sc_s[:, 64:96],
                                           in0=sc_s[:, 64:96], scalar1=ra,
                                           scalar2=None, op0=ALU.mult)
                sc.add("v", q_rs2, reads=["sc_r"] + rr, writes=["sc_r"])

            def q_u(e, src=in_ap):
                return e.tensor_tensor(
                    out=qa_s[:].rearrange("p (a b) -> p a b", b=32),
                    in0=src.rearrange("p (a b) -> p a b", b=32),
                    in1=_bc3(sc_s[:, 64:96], 32, 32), op=ALU.mult)
            sc.add(heavy, q_u, reads=[in_tok, "sc_r"], writes=["qa"])

        def emit_quantize_tail(out_tok, out_ap, heavy="g",
                               m1=None, m2=None, m3=None):
            m1ap, m1t = m1 if m1 else (mb1_s[:], ["mb1"])
            m2ap, m2t = m2 if m2 else (mb2_s[:], ["mb2"])
            m3ap, m3t = m3 if m3 else (mb3_s[:], ["mb3"])
            def q_au(e):
                return e.activation(out=m1ap, in_=qa_s[:], func=AF.Abs)
            sc.add("a", q_au, reads=["qa"], writes=m1t)

            def q_m1(e):
                return e.tensor_scalar(out=m2ap, in0=m1ap, scalar1=2.0,
                                       scalar2=None, op0=ALU.is_ge)
            sc.add("v", q_m1, reads=m1t, writes=m2t)

            def q_m2(e):
                return e.tensor_scalar(out=m1ap, in0=m1ap, scalar1=4.0,
                                       scalar2=None, op0=ALU.is_ge)
            sc.add("v", q_m2, reads=m1t, writes=m1t)

            def q_D(e):
                return e.tensor_scalar(out=m2ap, in0=m2ap, scalar1=-4.0,
                                       scalar2=8.0, op0=ALU.mult, op1=ALU.add)
            sc.add("v", q_D, reads=m2t, writes=m2t)

            def q_rq(e):
                return e.scalar_tensor_tensor(out=m3ap, in0=m1ap,
                                              scalar=-2.0, in1=m2ap,
                                              op0=ALU.mult, op1=ALU.add)
            sc.add("v", q_rq, reads=m1t + m2t, writes=m3t)

            def q_t(e):
                return e.tensor_tensor(out=qa_s[:], in0=qa_s[:], in1=m3ap,
                                       op=ALU.mult)
            sc.add(heavy, q_t, reads=["qa"] + m3t, writes=["qa"])

            def q_r1(e):
                return e.tensor_scalar(out=qi_s[:], in0=qa_s[:, 0:512],
                                       scalar1=0.0, scalar2=None, op0=ALU.add)
            sc.add("v", q_r1, reads=["qa"], writes=["qi"])

            def q_rf1(e):
                return e.tensor_scalar(out=qa_s[:, 0:512], in0=qi_s[:],
                                       scalar1=0, scalar2=None, op0=ALU.add)
            sc.add("v", q_rf1, reads=["qi"], writes=["qa"])

            def q_r2(e):
                return e.tensor_scalar(out=qi_s[:], in0=qa_s[:, 512:D],
                                       scalar1=0.0, scalar2=None, op0=ALU.add)
            sc.add("v", q_r2, reads=["qa"], writes=["qi"])

            def q_T2(e):
                return e.tensor_scalar(out=m2ap, in0=m2ap,
                                       scalar1=-1.0 / 32.0, scalar2=0.375,
                                       op0=ALU.mult, op1=ALU.add)
            sc.add("v", q_T2, reads=m2t, writes=m2t)

            def q_qtm(e):
                return e.scalar_tensor_tensor(out=m2ap, in0=m1ap,
                                              scalar=0.25, in1=m2ap,
                                              op0=ALU.mult, op1=ALU.add)
            sc.add("v", q_qtm, reads=m1t + m2t, writes=m2t)

            def q_rf2(e):
                return e.tensor_scalar(out=qa_s[:, 512:D], in0=qi_s[:],
                                       scalar1=0, scalar2=None, op0=ALU.add)
            sc.add("v", q_rf2, reads=["qi"], writes=["qa"])

            def q_w1(e):
                return e.tensor_tensor(
                    out=qa_s[:].rearrange("p (a b) -> p a b", b=32),
                    in0=qa_s[:].rearrange("p (a b) -> p a b", b=32),
                    in1=_bc3(sc_s[:, 32:64], 32, 32), op=ALU.mult)
            sc.add(heavy, q_w1, reads=["qa", "sc_s"], writes=["qa"])

            def q_xd(e, dst=out_ap):
                return e.tensor_tensor(out=dst, in0=qa_s[:], in1=m2ap,
                                       op=ALU.mult)
            sc.add(heavy, q_xd, reads=["qa"] + m2t, writes=[out_tok])

        def emit_wd_chunk(mc):
            sched_ld("s", x2_s[:, 0:D], wffn_d[mc * P:(mc + 1) * P, :],
                     "ld", ld_n, writes=["x2_0"])
            emit_quantize_head("x2_0", x2_s[:, 0:D], heavy="g")
            emit_quantize_tail("xd_0", xd_sb[:, 0:D], heavy="g",
                               m1=(pt_sb[:, 0:1024], ["ptr0", "ptr1"]),
                               m2=(pt_sb[:, 1024:2048], ["ptr2", "ptr3"]),
                               m3=(xdt_sb[:], ["xdt"]))
            for g in range(2):
                dst = wdt_s[:].rearrange("p (kc c) -> p kc c", c=D)[
                    :, 4 * g:4 * g + 4, mc * P:(mc + 1) * P]
                sched_tr4(xd_sb[:, g * 512:(g + 1) * 512], "xd_0",
                          dst, "wdt", g)

        sched_dump("dbg_wq", wqkv_s[:], ["wqkv_qk", "wqkv_vo"])
        sched_dump("dbg_wdt", wdt_s[:], ["wdt"])

        savedB = None
        for b in range(B_CORE):
            def a_ln(t, b=b):
                xp = t % 2
                xtok = f"x_{xp}"
                sched_ld("s", x_in[:, xp * D:(xp + 1) * D],
                         x_d[b, t * P:(t + 1) * P, :], "ld", ld_n,
                         writes=[xtok])
                sp = 32 * (t % 2)
                stm = f"stm_{t % 2}"
                stv = f"stv_{t % 2}"
                xcb = mb2_s if t % 2 == 0 else mb3_s
                xct = "mb2" if t % 2 == 0 else "mb3"

                def a_sum(e, xp=xp, sp=sp):
                    return e.reduce_sum(out=stat_s[:, sp:sp + 1],
                                        in_=x_in[:, xp * D:(xp + 1) * D],
                                        axis=AX.X)
                sc.add("v", a_sum, reads=[xtok], writes=[stm])

                def a_m(e, sp=sp):
                    return e.tensor_scalar(out=stat_s[:, sp + 1:sp + 2],
                                           in0=stat_s[:, sp:sp + 1],
                                           scalar1=1.0 / D, scalar2=None,
                                           op0=ALU.mult)
                sc.add("v", a_m, reads=[stm], writes=[stm])

                def a_xc(e, xp=xp, sp=sp, xcb=xcb):
                    return e.tensor_scalar(out=xcb[:],
                                           in0=x_in[:, xp * D:(xp + 1) * D],
                                           scalar1=stat_s[:, sp + 1:sp + 2],
                                           scalar2=None, op0=ALU.subtract)
                sc.add("v", a_xc, reads=[xtok, stm], writes=[xct])

                def a_sq(e, sp=sp, xcb=xcb):
                    return e.scalar_tensor_tensor(
                        out=mb1_s[:], in0=xcb[:], scalar=1.0, in1=xcb[:],
                        op0=ALU.mult, op1=ALU.mult,
                        accum_out=stat_s[:, sp + 2:sp + 3])
                sc.add("v", a_sq, reads=[xct], writes=["mb1", stv])

                def a_rstd(e, sp=sp):
                    e.activation(out=stat_s[:, sp + 3:sp + 4],
                                 in_=stat_s[:, sp + 2:sp + 3], func=AF.Ln,
                                 scale=1.0 / D, bias=EPS_LN)
                    e.drain()
                    return e.activation(out=stat_s[:, sp + 4:sp + 5],
                                        in_=stat_s[:, sp + 3:sp + 4],
                                        func=AF.Exp, scale=-0.5)
                sc.add("a", a_rstd, reads=[stv, "eps"], writes=[stv])

                def a_diag(e, sp=sp, dgb=(diag_s if t % 2 == 0 else diagb_s)):
                    return e.tensor_scalar(out=dgb[:], in0=ident_s[:],
                                           scalar1=stat_s[:, sp + 4:sp + 5],
                                           scalar2=None, op0=ALU.mult)
                sc.add("v", a_diag, reads=["ident", stv],
                       writes=[f"diag_{t % 2}"])

            def a_pe(t, b=b):
                xp = t % 2
                xcb = mb2_s if t % 2 == 0 else mb3_s
                xct = "mb2" if t % 2 == 0 else "mb3"
                xnt_view = p_sb[:, xp * D:(xp + 1) * D]
                for g in range(2):
                    j = g % 2
                    tok = f"tr{j}"

                    def a_trx(e, g=g, j=j, xcb=xcb,
                              dgb=(diag_s if t % 2 == 0 else diagb_s)):
                        r = None
                        for kc in range(4):
                            r = e.matmul(
                                out=tr_ps2[j][:, kc * P:(kc + 1) * P],
                                lhsT=xcb[:, (g * 4 + kc) * P:
                                          (g * 4 + kc + 1) * P],
                                rhs=dgb[:], start=True, stop=True)
                        return r
                    sc.add("t", a_trx, reads=[xct, f"diag_{t % 2}"],
                           writes=[tok])

                    def a_evx(e, dst=xnt_view, g=g, j=j):
                        return e.activation(out=dst[:, g * 512:(g + 1) * 512],
                                            in_=tr_ps2[j][:], func=AF.Copy)
                    sc.add("a", a_evx, reads=[tok],
                           writes=[f"p{xp * 2 + g}"])

            def a_mm_qk(t, b=b):
                ptoks = [f"p{(t % 2) * 2}", f"p{(t % 2) * 2 + 1}"]
                xnt_view = p_sb[:, (t % 2) * D:(t % 2 + 1) * D]
                for ocg in range(4):
                    def a_qk(e, ocg=ocg, src=xnt_view):
                        r = None
                        for j in range(4):
                            oc = ocg * 4 + j
                            for kc in range(DC):
                                r = e.matmul(
                                    out=s_ps[ocg][:, j * P:(j + 1) * P],
                                    lhsT=wqkv_s[:, kc * 3 * D + oc * P:
                                                kc * 3 * D + oc * P + P],
                                    rhs=src[:, kc * P:(kc + 1) * P],
                                    start=(kc == 0), stop=(kc == DC - 1))
                        return r
                    sc.add("t", a_qk, reads=ptoks + ["wqkv_qk"],
                           writes=[f"s{ocg}"])

                    def a_evqk(e, ocg=ocg, t=t):
                        dstbase = qt_s if ocg < 2 else kt_s
                        oc0 = (ocg * 4) % 8
                        dst = dstbase[:].rearrange("p (a c) -> p a c", c=NTOK)[
                            :, oc0:oc0 + 4, t * P:(t + 1) * P]
                        return e.activation(
                            out=dst,
                            in_=s_ps[ocg][:].rearrange("p (a c) -> p a c", c=P),
                            func=AF.Copy)
                    sc.add("a", a_evqk, reads=[f"s{ocg}"],
                           writes=(["qt"] if ocg < 2 else
                                   ["kt", "kt0", "kt1"]))

            def a_mm_vo(t, b=b):
                ptoks = [f"p{(t % 2) * 2}", f"p{(t % 2) * 2 + 1}"]
                xnt_view = p_sb[:, (t % 2) * D:(t % 2 + 1) * D]

                def a_vo(e, src=xnt_view):
                    r = None
                    for nh in range(2):
                        for kc in range(DC):
                            r = e.matmul(
                                out=of_ps[nh][:],
                                lhsT=src[:, kc * P:(kc + 1) * P],
                                rhs=wqkv_s[:, kc * 3 * D + 2 * D + nh * 512:
                                           kc * 3 * D + 2 * D + nh * 512 + 512],
                                start=(kc == 0), stop=(kc == DC - 1))
                    return r
                sc.add("t", a_vo, reads=ptoks + ["wqkv_vo"],
                       writes=["of0", "of1"])

                def a_evvo(e, t=t):
                    e.activation(out=vo_s[:, t * D:t * D + 512],
                                 in_=of_ps[0][:], func=AF.Copy)
                    e.activation(out=vo_s[:, t * D + 512:(t + 1) * D],
                                 in_=of_ps[1][:], func=AF.Copy)
                    return None
                sc.add("a", a_evvo, reads=["of0", "of1"], writes=["vo"])

            def emit_A_tile(t, bb=b):
                if t + 2 < NT:
                    a_ln(t + 2, bb)
                a_mm_qk(t, bb)
                if t + 1 < NT:
                    a_pe(t + 1, bb)
                a_mm_vo(t, bb)


            if b == 0:
                sched_dump("dbg_qt", qt_s[:], ["qt"])
                sched_dump("dbg_kt", kt_s[:], ["kt"])
                sched_dump("dbg_vo", vo_s[:], ["vo"])

            def stage_S1(i, b=b):
                xp = i % 2
                sp = 32 * (i % 2)
                stx = f"stx_{i % 2}"
                sched_ld("s", x_in[:, xp * D:(xp + 1) * D],
                         x_d[b, i * P:(i + 1) * P, :], "ld", ld_n,
                         writes=[f"x_{xp}"])

                def b_xpb(e, xp=xp):
                    v = x_in[:, xp * D:(xp + 1) * D]
                    return e.tensor_tensor(out=v, in0=v, in1=outb_s[:],
                                           op=ALU.add)
                sc.add("g", b_xpb, reads=[f"x_{xp}", "outb"],
                       writes=[f"x_{xp}"])

                for nb in range(4):
                    def b_s(e, i=i, nb=nb):
                        r = None
                        for kc in range(DC):
                            r = e.matmul(
                                out=s_ps[nb][:],
                                lhsT=qt_s[:, kc * NTOK + i * P:
                                          kc * NTOK + i * P + P],
                                rhs=kt_s[:, kc * NTOK + nb * 512:
                                         kc * NTOK + nb * 512 + 512],
                                start=(kc == 0), stop=(kc == DC - 1))
                        return r
                    sc.add("t", b_s, reads=["qt", "kt"], writes=[f"s{nb}"])

                def b_max(e, sp=sp):
                    for nb in range(4):
                        e.reduce_max(out=stat_s[:, sp + 8 + nb:sp + 9 + nb],
                                     in_=s_ps[nb][:], axis=AX.X, negate=True)
                    e.drain()
                    return e.tensor_reduce(out=stat_s[:, sp + 12:sp + 13],
                                           in_=stat_s[:, sp + 8:sp + 12],
                                           axis=AX.X, op=ALU.min)
                sc.add("v", b_max, reads=["s0", "s1", "s2", "s3"],
                       writes=[stx])

            def stage_S2(i, b=b):
                sp = 32 * (i % 2)
                stx = f"stx_{i % 2}"
                std = f"std_{i % 2}"

                for nb in range(4):
                    def b_exp(e, sp=sp, nb=nb):
                        return e.activation(
                            out=p_sb[:, nb * 512:(nb + 1) * 512],
                            in_=s_ps[nb][:], func=AF.Exp,
                            bias=stat_s[:, sp + 12:sp + 13], scale=1.0,
                            accum_out=stat_s[:, sp + 16 + nb:sp + 17 + nb])
                    sc.add("a", b_exp, reads=[f"s{nb}", stx],
                           writes=[f"p{nb}", f"std{nb}_{i % 2}"])

                def b_dengather(e, sp=sp):
                    return e.tensor_copy(out=stat_s[:, sp + 22:sp + 23],
                                         in_=stat_s[:, sp + 16:sp + 17])
                sc.add("v", b_dengather,
                       reads=[f"std0_{i % 2}", f"std1_{i % 2}",
                              f"std2_{i % 2}", f"std3_{i % 2}"],
                       writes=[std])

                def b_den(e, sp=sp):
                    e.reduce_sum(out=stat_s[:, sp + 20:sp + 21],
                                 in_=stat_s[:, sp + 16:sp + 20], axis=AX.X)
                    e.drain()
                    return e.reciprocal(out=stat_s[:, sp + 21:sp + 22],
                                        in_=stat_s[:, sp + 20:sp + 21])
                sc.add("v", b_den, reads=[std], writes=[std])

            def stage_trp(i, q, b=b):
                sched_tr4(p_sb[:, (q * 4) * P:(q * 4 + 4) * P], f"p{q}",
                          pt_sb[:, q * 512:(q + 1) * 512], f"ptr{q}", q)

            def stage_o(i, q, b=b):
                def b_o(e, q=q):
                    rr = None
                    for j in range(4):
                        kb = q * 4 + j
                        for nh in range(2):
                            rr = e.matmul(
                                out=of_ps[nh][:],
                                lhsT=pt_sb[:, kb * P:(kb + 1) * P],
                                rhs=vo_s[:, kb * D + nh * 512:
                                         kb * D + nh * 512 + 512],
                                start=(kb == 0), stop=(kb == NT - 1))
                    return rr
                sc.add("t", b_o, reads=[f"ptr{q}", "vo"],
                       writes=["of0", "of1"])

            def stage_Oep(i, b=b):
                xp = i % 2
                sp = 32 * (i % 2)
                stm = f"stm_{i % 2}"
                stv = f"stv_{i % 2}"
                std = f"std_{i % 2}"
                x2tok = f"x2_{i % 2}"
                x2v = x2_s[:, (i % 2) * D:(i % 2 + 1) * D]

                def b_oep(e, xp=xp, sp=sp, x2v=x2v):
                    for nh in range(2):
                        e.scalar_tensor_tensor(
                            out=x2v[:, nh * 512:(nh + 1) * 512],
                            in0=of_ps[nh][:],
                            scalar=stat_s[:, sp + 21:sp + 22],
                            in1=x_in[:, xp * D + nh * 512:
                                     xp * D + nh * 512 + 512],
                            op0=ALU.mult, op1=ALU.add)
                    return None
                sc.add("v", b_oep, reads=["of0", "of1", std, f"x_{xp}"],
                       writes=[x2tok])

                def b_nsum(e, x2v=x2v, sp=sp):
                    return e.reduce_sum(out=stat_s[:, sp + 0:sp + 1], in_=x2v,
                                        axis=AX.X)
                sc.add("v", b_nsum, reads=[x2tok], writes=[stm])

                def b_sq2(e, x2v=x2v, sp=sp):
                    return e.activation(out=mb1_s[:], in_=x2v, func=AF.Square,
                                        accum_out=stat_s[:, sp + 2:sp + 3])
                sc.add("a", b_sq2, reads=[x2tok], writes=["mb1", stv])

                def b_negm(e, sp=sp):
                    return e.tensor_scalar(out=stat_s[:, sp + 1:sp + 2],
                                           in0=stat_s[:, sp + 0:sp + 1],
                                           scalar1=1.0 / D, scalar2=None,
                                           op0=ALU.mult)
                sc.add("v", b_negm, reads=[stm], writes=[stm])

                def b_xc2(e, x2v=x2v, sp=sp):
                    return e.tensor_scalar(out=qa_s[:], in0=x2v,
                                           scalar1=stat_s[:, sp + 1:sp + 2],
                                           scalar2=None, op0=ALU.subtract)
                sc.add("v", b_xc2, reads=[x2tok, stm], writes=["qa"])

                def b_var(e, sp=sp):
                    e.tensor_scalar(out=stat_s[:, sp + 6:sp + 7],
                                    in0=stat_s[:, sp + 1:sp + 2],
                                    scalar1=stat_s[:, sp + 1:sp + 2],
                                    scalar2=None, op0=ALU.mult)
                    e.drain()
                    return e.tensor_scalar(out=stat_s[:, sp + 5:sp + 6],
                                           in0=stat_s[:, sp + 2:sp + 3],
                                           scalar1=1.0 / D,
                                           scalar2=stat_s[:, sp + 6:sp + 7],
                                           op0=ALU.mult, op1=ALU.subtract)
                sc.add("v", b_var, reads=[stv, stm], writes=[stv])

                def b_rstd(e, sp=sp):
                    e.activation(out=stat_s[:, sp + 3:sp + 4],
                                 in_=stat_s[:, sp + 5:sp + 6], func=AF.Ln,
                                 bias=EPS_LN)
                    e.drain()
                    return e.activation(out=stat_s[:, sp + 4:sp + 5],
                                        in_=stat_s[:, sp + 3:sp + 4],
                                        func=AF.Exp, scale=-0.5)
                sc.add("a", b_rstd, reads=[stv, "eps"], writes=[stv])

                emit_quantize_head("qa", qa_s[:],
                                   rstd_ap=stat_s[:, sp + 4:sp + 5],
                                   rstd_tok=stv)

            def stage_QT(i, b=b):
                emit_quantize_tail(f"xd_{i % 2}",
                                   xd_sb[:, (i % 2) * D:(i % 2 + 1) * D])

            def stage_trxd(i, b=b):
                xdv = xd_sb[:, (i % 2) * D:(i % 2 + 1) * D]
                for g in range(2):
                    sched_tr4(xdv[:, g * 512:(g + 1) * 512], f"xd_{i % 2}",
                              xdt_sb[:, g * 512:(g + 1) * 512], "xdt", g)

            def stage_Fb(i, b=b):
                x2tok = f"x2_{i % 2}"
                x2v = x2_s[:, (i % 2) * D:(i % 2 + 1) * D]

                def b_addfb(e, x2v=x2v):
                    return e.tensor_tensor(out=x2v, in0=x2v, in1=ffnb_s[:],
                                           op=ALU.add)
                sc.add("g", b_addfb, reads=[x2tok, "ffnb"], writes=[x2tok])

            def stage_F(i, half, b=b):
                x2tok = f"x2_{i % 2}"
                x2v = x2_s[:, (i % 2) * D:(i % 2 + 1) * D]

                def b_f(e, half=half):
                    r = None
                    for kc in range(DC):
                        r = e.matmul(
                            out=of_ps[half][:],
                            lhsT=xdt_sb[:, kc * P:(kc + 1) * P],
                            rhs=wdt_s[:, kc * D + half * 512:
                                      kc * D + half * 512 + 512],
                            start=(kc == 0), stop=(kc == DC - 1))
                    return r
                sc.add("t", b_f, reads=["xdt", "wdt"], writes=[f"of{half}"])

                def b_fep(e, x2v=x2v, half=half):
                    return e.tensor_tensor(
                        out=x2v[:, half * 512:(half + 1) * 512],
                        in0=x2v[:, half * 512:(half + 1) * 512],
                        in1=of_ps[half][:], op=ALU.add)
                sc.add("v", b_fep, reads=[f"of{half}", x2tok], writes=[x2tok])

            def stage_sto(i, b=b):
                x2tok = f"x2_{i % 2}"
                x2v = x2_s[:, (i % 2) * D:(i % 2 + 1) * D]
                sto_n[0] += 16
                tgt = sto_n[0]

                def b_sto(e, b=b, i=i, x2v=x2v):
                    e.dma_start(out=out_d[b, i * P:(i + 1) * P, :],
                                in_=x2v).then_inc(sem["sto"], 16)
                    return None
                sc.add("s", b_sto, reads=[x2tok], dma=("sto", tgt))

            def emit_B_iter(i, _f=(stage_S1, stage_trxd, stage_Fb,
                                   stage_trp, stage_F, stage_sto, stage_QT,
                                   stage_S2, stage_o, stage_Oep)):
                (f_S1, f_trxd, f_Fb, f_trp, f_F, f_sto, f_QT, f_S2, f_o,
                 f_Oep) = _f
                if i < NT:
                    f_S1(i)
                if i - 3 >= 0:
                    f_trxd(i - 3)
                    f_Fb(i - 3)
                if 0 <= i - 1 < NT:
                    f_trp(i - 1, 0)
                    f_trp(i - 1, 1)
                if i - 3 >= 0:
                    f_F(i - 3, 0)
                if 0 <= i - 1 < NT:
                    f_trp(i - 1, 2)
                    f_trp(i - 1, 3)
                if i - 3 >= 0:
                    f_F(i - 3, 1)
                    f_sto(i - 3)
                if 0 <= i - 2 < NT:
                    f_QT(i - 2)
                if i < NT:
                    f_S2(i)
                if 0 <= i - 1 < NT:
                    for q in range(4):
                        f_o(i - 1, q)
                if 0 <= i - 1 < NT:
                    f_Oep(i - 1)

            if b > 0:
                prev_iter = savedB[0]
                for i in range(NT, NT + 3):
                    prev_iter(i)

            a_ln(0)
            a_ln(1)
            a_pe(0)
            for t in range(NT):
                emit_A_tile(t)
                if b == 0 and t % 2 == 0:
                    emit_wd_chunk(t // 2)

            for i in range(NT):
                emit_B_iter(i)
            savedB = (emit_B_iter,)

        for i in range(NT, NT + 3):
            savedB[0](i)

        total_sto = sto_n[0]

        def _fin(e):
            e.wait_ge(sem["sto"], total_sto)
            return None
        sc.add("s", _fin)

        steps = sc.steps
        for s_ in steps:
            for d in s_["deps"]:
                steps[d]["need"] = True
        mcount = {}
        for s_ in steps:
            if s_["dma"] is None and s_["need"]:
                en = s_["eng"]
                mcount[en] = mcount.get(en, 0) + 1
                s_["mno"] = mcount[en]

        with nc.Block() as block:
            def runner(name):
                def run(eng):
                    covered = {}
                    self_cov = 0
                    for k, s_ in enumerate(steps):
                        if s_["eng"] != name:
                            continue
                        waits = {}
                        for d in sorted(s_["deps"]):
                            ds = steps[d]
                            if ds["dma"] is not None:
                                smn, tgt = ds["dma"]
                                if covered.get(smn, 0) < tgt:
                                    waits[smn] = max(waits.get(smn, 0), tgt)
                            elif ds["eng"] == name:
                                assert ds["mno"] is not None, (k, d)
                                if ds["mno"] > self_cov:
                                    waits[name] = max(waits.get(name, 0),
                                                      ds["mno"])
                            else:
                                assert ds["mno"] is not None, (k, d, ds)
                                if covered.get(ds["eng"], 0) < ds["mno"]:
                                    waits[ds["eng"]] = max(
                                        waits.get(ds["eng"], 0), ds["mno"])
                        for smn, v in waits.items():
                            eng.wait_ge(sem[smn], v)
                            if smn == name:
                                self_cov = max(self_cov, v)
                            else:
                                covered[smn] = max(covered.get(smn, 0), v)
                        r = s_["fn"](eng)
                        if s_["dma"] is None and s_["mno"] is not None:
                            if r is None:
                                eng.drain().then_inc(sem[name], 1)
                            else:
                                r.then_inc(sem[name], 1)
                return run
            block.vector(runner("v"))
            block.scalar(runner("a"))
            block.tensor(runner("t"))
            block.gpsimd(runner("g"))
            block.sync(runner("s"))

    return nc


_NC_CACHE = None


def kernel(L0_attn_x, L0_attn_weight_qkv, L0_attn_out_w, L0_attn_out_b,
           L0_ffn_weight, L0_ffn_bias):
    global _NC_CACHE
    x = np.ascontiguousarray(np.asarray(L0_attn_x, dtype=np.float32))
    wqkv = np.ascontiguousarray(np.asarray(L0_attn_weight_qkv, dtype=np.float32))
    out_w = np.asarray(L0_attn_out_w, dtype=np.float32)
    out_b = np.asarray(L0_attn_out_b, dtype=np.float32)
    wffn = np.ascontiguousarray(np.asarray(L0_ffn_weight, dtype=np.float32))
    ffnb = np.asarray(L0_ffn_bias, dtype=np.float32)

    wvt = np.ascontiguousarray(wqkv[:, 2 * D:3 * D].T)
    owt = np.ascontiguousarray(out_w.T)
    outb_bc = np.ascontiguousarray(np.broadcast_to(out_b, (P, D)))
    ffnb_bc = np.ascontiguousarray(np.broadcast_to(ffnb, (P, D)))
    ident = np.eye(P, dtype=ml_dtypes.bfloat16)

    if _NC_CACHE is None:
        _NC_CACHE = build_program()
    nc = _NC_CACHE

    in_maps = []
    for c in range(N_CORES):
        in_maps.append({
            "x": np.ascontiguousarray(x[c * B_CORE:(c + 1) * B_CORE]),
            "wqkv": wqkv,
            "wvt": wvt,
            "owt": owt,
            "outb": outb_bc,
            "wffn": wffn,
            "ffnb": ffnb_bc,
            "ident": ident,
        })
    res = run_bass_kernel_spmd(nc, in_maps, core_ids=list(range(N_CORES)))
    out = np.concatenate([res.results[c]["out"] for c in range(N_CORES)], axis=0)
    return out.astype(np.float32)


# version 31
# speedup vs baseline: 1.1219x; 1.0573x over previous
import numpy as np
import ml_dtypes
from contextlib import ExitStack

import concourse.bass as bass
import concourse.mybir as mybir
from concourse.bass_utils import run_bass_kernel_spmd

F32 = mybir.dt.float32
BF16 = mybir.dt.bfloat16
I32 = mybir.dt.int32
AF = mybir.ActivationFunctionType
ALU = mybir.AluOpType
AX = mybir.AxisListType

P = 128
D = 1024
DC = 8
NTOK = 2048
NT = 16
B_CORE = 4
N_CORES = 8
EPS_LN = 1e-5
EPS_Q = 1e-30
C_7_5 = float(np.float32(1.0 / 7.5))


def _bc3(ap2d, blocks, width):
    return ap2d.rearrange("p (a b) -> p a b", b=1).broadcast_to([P, blocks, width])


class Sched:

    def __init__(self):
        self.steps = []
        self.writer = {}
        self.readers = {}

    def add(self, eng, fn, reads=(), writes=(), dma=None):
        deps = set()
        for tok in reads:
            w = self.writer.get(tok)
            if w is not None:
                deps.add(w)
        for tok in writes:
            w = self.writer.get(tok)
            if w is not None:
                deps.add(w)
            for r in self.readers.get(tok, ()):
                deps.add(r)
        k = len(self.steps)
        for tok in reads:
            self.readers.setdefault(tok, []).append(k)
        for tok in writes:
            self.writer[tok] = k
            self.readers[tok] = []
        self.steps.append(dict(eng=eng, fn=fn, deps=deps, dma=dma, need=False,
                               mno=None))
        return k


def build_program(debug=False):
    nc = bass.Bass("TRN2", target_bir_lowering=False, debug=False)

    x_d = nc.dram_tensor("x", [B_CORE, NTOK, D], F32, kind="ExternalInput").ap()
    wqkv_d = nc.dram_tensor("wqkv", [D, 3 * D], F32, kind="ExternalInput").ap()
    wvt_d = nc.dram_tensor("wvt", [D, D], F32, kind="ExternalInput").ap()
    owt_d = nc.dram_tensor("owt", [D, D], F32, kind="ExternalInput").ap()
    outb_d = nc.dram_tensor("outb", [P, D], F32, kind="ExternalInput").ap()
    wffn_d = nc.dram_tensor("wffn", [D, D], F32, kind="ExternalInput").ap()
    ffnb_d = nc.dram_tensor("ffnb", [P, D], F32, kind="ExternalInput").ap()
    ident_d = nc.dram_tensor("ident", [P, P], BF16, kind="ExternalInput").ap()
    out_d = nc.dram_tensor("out", [B_CORE, NTOK, D], F32, kind="ExternalOutput").ap()

    with ExitStack() as st:
        def sb(name, shape, dtype):
            return st.enter_context(nc.sbuf_tensor(name, shape, dtype))

        wqkv_s = sb("wqkv_s", [P, DC * 3 * D], BF16)
        wdt_s = sb("wdt_s", [P, DC * D], BF16)
        qt_s = sb("qt_s", [P, DC * NTOK], BF16)
        kt_s = sb("kt_s", [P, DC * NTOK], BF16)
        vo_s = sb("vo_s", [P, NT * D], BF16)
        x_in = sb("x_in", [P, 2 * D], F32)
        outb_s = sb("outb_s", [P, D], BF16)
        ffnb_s = sb("ffnb_s", [P, D], BF16)
        ident_s = sb("ident_s", [P, P], BF16)
        diag_s = sb("diag_s", [P, P], BF16)
        diagb_s = sb("diagb_s", [P, P], BF16)
        stat_s = sb("stat_s", [P, 64], F32)
        sc_s = sb("sc_s", [P, 96], F32)
        p_sb = sb("p_sb", [P, NTOK], BF16)
        pt_sb = sb("pt_sb", [P, 16 * P], BF16)
        mb3_s = sb("mb3_s", [P, D], BF16)
        x2_s = sb("x2_s", [P, 2 * D], F32)
        qa_s = sb("qa_s", [P, D], F32)
        qi_s = sb("qi_s", [P, D // 2], I32)
        mb1_s = sb("mb1_s", [P, D], BF16)
        mb2_s = sb("mb2_s", [P, D], BF16)
        xd_sb = sb("xd_sb", [P, 2 * D], BF16)
        xdt_sb = sb("xdt_sb", [P, 8 * P], BF16)
        eps_s = sb("eps_s", [P, 1], F32)
        ones_r = sb("ones_r", [1, P], BF16)

        s_ps = [st.enter_context(nc.psum_tensor(f"s_ps{j}", [P, 512], F32))
                for j in range(4)]
        of_ps = [st.enter_context(nc.psum_tensor(f"of_ps{j}", [P, 512], F32))
                 for j in range(2)]
        tr_ps2 = [st.enter_context(nc.psum_tensor(f"tr_ps{j}", [P, 4 * P], F32))
                  for j in range(2)]

        sem = {n: st.enter_context(nc.semaphore("sem_" + n))
               for n in ("t", "v", "a", "g", "ld", "gld", "sto")}

        sc = Sched()
        ld_n = [0]
        gld_n = [0]
        sto_n = [0]

        def sched_ld(eng, dst, src, semname, ctr, reads=(), writes=()):
            ctr[0] += 16
            tgt = ctr[0]

            def fn(e, dst=dst, src=src, sm=sem[semname]):
                e.dma_start(out=dst, in_=src).then_inc(sm, 16)
                return None
            return sc.add(eng, fn, reads=reads, writes=writes,
                          dma=(semname, tgt))

        dbg = {}
        if debug:
            for nm, shape, dt in [
                ("dbg_wq", [P, DC * 3 * D], BF16), ("dbg_wdt", [P, DC * D], BF16),
                ("dbg_qt", [P, DC * NTOK], BF16), ("dbg_kt", [P, DC * NTOK], BF16),
                ("dbg_vo", [P, NT * D], BF16),
            ]:
                dbg[nm] = nc.dram_tensor(nm, shape, dt, kind="ExternalOutput").ap()

        def sched_dump(name, src_ap, reads):
            if not debug or name not in dbg:
                return
            sto_n[0] += 16
            tgt = sto_n[0]

            def fn(e, d=dbg[name], s=src_ap):
                e.dma_start(out=d, in_=s).then_inc(sem["sto"], 16)
                return None
            sc.add("s", fn, reads=reads, dma=("sto", tgt))

        def sched_tr4(src_ap, src_tok, dst_ap, dst_tok, grp, extra_reads=()):
            j = grp % 2
            tok = f"tr{j}"

            def fn_t(e, s=src_ap, j=j):
                r = None
                for kc in range(4):
                    r = e.matmul(out=tr_ps2[j][:, kc * P:(kc + 1) * P],
                                 lhsT=s[:, kc * P:(kc + 1) * P],
                                 rhs=ident_s[:], start=True, stop=True)
                return r
            sc.add("t", fn_t, reads=[src_tok, "ident"] + list(extra_reads),
                   writes=[tok])

            def fn_a(e, d=dst_ap, j=j):
                return e.activation(out=d, in_=tr_ps2[j][:], func=AF.Copy)
            sc.add("a", fn_a, reads=[tok], writes=[dst_tok])

        def _act_preload(e):
            return e.add_instruction(mybir.InstLoadActFuncSet(
                name=nc.get_next_instruction_name(), act_func_set_id=6,
                ins=[], outs=[]))
        sc.add("a", _act_preload)

        nc.const_aps.register(F32, EPS_LN, eps_s[:])

        def _eps_set(e):
            return e.memset(eps_s[:], EPS_LN)
        sc.add("v", _eps_set, writes=["eps"])

        def _ones_set(e):
            return e.memset(ones_r[:], 1.0)
        sc.add("v", _ones_set, writes=["ones"])

        sched_ld("s", ident_s[:], ident_d, "ld", ld_n, writes=["ident"])
        sched_ld("g", outb_s[:], outb_d, "gld", gld_n, writes=["outb"])
        sched_ld("g", ffnb_s[:], ffnb_d, "gld", gld_n, writes=["ffnb"])
        for half in range(2):
            src = wqkv_d[:, half * D:(half + 1) * D].rearrange(
                "(kc p) c -> p kc c", p=P)
            dst = wqkv_s[:].rearrange("p (kc c) -> p kc c", c=3 * D)[
                :, :, half * D:(half + 1) * D]
            sched_ld("g", dst, src, "gld", gld_n, writes=["wqkv_qk"])
        owt_v = qt_s[:, 0:DC * D].rearrange("p (kc c) -> p kc c", c=D)
        sched_ld("g", owt_v, owt_d.rearrange("(kc p) c -> p kc c", p=P),
                 "gld", gld_n, writes=["qt"])

        def _scale_wq(e):
            v = wqkv_s[:].rearrange("p (kc c) -> p kc c", c=3 * D)[:, :, 0:D]
            return e.tensor_scalar(out=v, in0=v, scalar1=1.0 / 32.0,
                                   scalar2=None, op0=ALU.mult)
        sc.add("v", _scale_wq, reads=["wqkv_qk"], writes=["wqkv_qk"])

        for m in range(DC):
            reg = (m % 2) * DC * P
            ktok = f"kt{m % 2}"
            wvt_v = kt_s[:, reg:reg + DC * P].rearrange(
                "p (kc c) -> p kc c", c=P)
            src = wvt_d[:, m * P:(m + 1) * P].rearrange("(kc p) c -> p kc c", p=P)
            sched_ld("g", wvt_v, src, "gld", gld_n, writes=[ktok])

            def _mm_wvo(e, reg=reg):
                r = None
                for nh in range(2):
                    for kc in range(DC):
                        r = e.matmul(
                            out=of_ps[nh][:],
                            lhsT=kt_s[:, reg + kc * P:reg + (kc + 1) * P],
                            rhs=qt_s[:, kc * D + nh * 512: kc * D + nh * 512 + 512],
                            start=(kc == 0), stop=(kc == DC - 1))
                return r
            sc.add("t", _mm_wvo, reads=[ktok, "qt"], writes=["of0", "of1"])

            def _ev_wvo(e, m=m):
                base = m * 3 * D + 2 * D
                e.activation(out=wqkv_s[:, base:base + 512], in_=of_ps[0][:],
                             func=AF.Copy)
                e.activation(out=wqkv_s[:, base + 512:base + D],
                             in_=of_ps[1][:], func=AF.Copy)
                return None
            sc.add("a", _ev_wvo, reads=["of0", "of1"], writes=["wqkv_vo"])

        def emit_quantize_head(in_tok, in_ap, rstd_ap=None, rstd_tok=None,
                               heavy="g"):
            rr = [rstd_tok] if rstd_tok else []

            def q_bmax(e, src=in_ap):
                return e.reduce_max(
                    out=sc_s[:, 0:32],
                    in_=src.rearrange("p (a b) -> p a b", b=32),
                    axis=AX.X, apply_absolute_value=True)
            sc.add("v", q_bmax, reads=[in_tok], writes=["sc_b"])

            if rstd_ap is not None:
                def q_bm2(e, ra=rstd_ap):
                    return e.tensor_scalar(out=sc_s[:, 0:32],
                                           in0=sc_s[:, 0:32], scalar1=ra,
                                           scalar2=None, op0=ALU.mult)
                sc.add("v", q_bm2, reads=["sc_b"] + rr, writes=["sc_b"])

            def q_scal(e):
                return e.tensor_scalar(out=sc_s[:, 32:64], in0=sc_s[:, 0:32],
                                       scalar1=C_7_5, scalar2=EPS_Q,
                                       op0=ALU.mult, op1=ALU.max)
            sc.add("v", q_scal, reads=["sc_b"], writes=["sc_s"])

            def q_rsc(e):
                return e.reciprocal(out=sc_s[:, 64:96], in_=sc_s[:, 32:64])
            sc.add("v", q_rsc, reads=["sc_s"], writes=["sc_r"])

            if rstd_ap is not None:
                def q_rs2(e, ra=rstd_ap):
                    return e.tensor_scalar(out=sc_s[:, 64:96],
                                           in0=sc_s[:, 64:96], scalar1=ra,
                                           scalar2=None, op0=ALU.mult)
                sc.add("v", q_rs2, reads=["sc_r"] + rr, writes=["sc_r"])

            def q_u(e, src=in_ap):
                return e.tensor_tensor(
                    out=qa_s[:].rearrange("p (a b) -> p a b", b=32),
                    in0=src.rearrange("p (a b) -> p a b", b=32),
                    in1=_bc3(sc_s[:, 64:96], 32, 32), op=ALU.mult)
            sc.add(heavy, q_u, reads=[in_tok, "sc_r"], writes=["qa"])

        def emit_quantize_tail(out_tok, out_ap, heavy="g",
                               m1=None, m2=None, m3=None):
            m1ap, m1t = m1 if m1 else (mb1_s[:], ["mb1"])
            m2ap, m2t = m2 if m2 else (mb2_s[:], ["mb2"])
            m3ap, m3t = m3 if m3 else (mb3_s[:], ["mb3"])
            def q_au(e):
                return e.activation(out=m1ap, in_=qa_s[:], func=AF.Abs)
            sc.add("a", q_au, reads=["qa"], writes=m1t)

            def q_m1(e):
                return e.tensor_scalar(out=m2ap, in0=m1ap, scalar1=2.0,
                                       scalar2=None, op0=ALU.is_ge)
            sc.add("v", q_m1, reads=m1t, writes=m2t)

            def q_m2(e):
                return e.tensor_scalar(out=m1ap, in0=m1ap, scalar1=4.0,
                                       scalar2=None, op0=ALU.is_ge)
            sc.add("v", q_m2, reads=m1t, writes=m1t)

            def q_D(e):
                return e.tensor_scalar(out=m2ap, in0=m2ap, scalar1=-4.0,
                                       scalar2=8.0, op0=ALU.mult, op1=ALU.add)
            sc.add("v", q_D, reads=m2t, writes=m2t)

            def q_rq(e):
                return e.scalar_tensor_tensor(out=m3ap, in0=m1ap,
                                              scalar=-2.0, in1=m2ap,
                                              op0=ALU.mult, op1=ALU.add)
            sc.add("v", q_rq, reads=m1t + m2t, writes=m3t)

            def q_t(e):
                return e.tensor_tensor(out=qa_s[:], in0=qa_s[:], in1=m3ap,
                                       op=ALU.mult)
            sc.add(heavy, q_t, reads=["qa"] + m3t, writes=["qa"])

            def q_r1(e):
                return e.tensor_scalar(out=qi_s[:], in0=qa_s[:, 0:512],
                                       scalar1=0.0, scalar2=None, op0=ALU.add)
            sc.add("v", q_r1, reads=["qa"], writes=["qi"])

            def q_rf1(e):
                return e.tensor_scalar(out=qa_s[:, 0:512], in0=qi_s[:],
                                       scalar1=0, scalar2=None, op0=ALU.add)
            sc.add("v", q_rf1, reads=["qi"], writes=["qa"])

            def q_r2(e):
                return e.tensor_scalar(out=qi_s[:], in0=qa_s[:, 512:D],
                                       scalar1=0.0, scalar2=None, op0=ALU.add)
            sc.add("v", q_r2, reads=["qa"], writes=["qi"])

            def q_T2(e):
                return e.tensor_scalar(out=m2ap, in0=m2ap,
                                       scalar1=-1.0 / 32.0, scalar2=0.375,
                                       op0=ALU.mult, op1=ALU.add)
            sc.add("v", q_T2, reads=m2t, writes=m2t)

            def q_qtm(e):
                return e.scalar_tensor_tensor(out=m2ap, in0=m1ap,
                                              scalar=0.25, in1=m2ap,
                                              op0=ALU.mult, op1=ALU.add)
            sc.add("v", q_qtm, reads=m1t + m2t, writes=m2t)

            def q_rf2(e):
                return e.tensor_scalar(out=qa_s[:, 512:D], in0=qi_s[:],
                                       scalar1=0, scalar2=None, op0=ALU.add)
            sc.add("v", q_rf2, reads=["qi"], writes=["qa"])

            def q_w1(e):
                return e.tensor_tensor(
                    out=qa_s[:].rearrange("p (a b) -> p a b", b=32),
                    in0=qa_s[:].rearrange("p (a b) -> p a b", b=32),
                    in1=_bc3(sc_s[:, 32:64], 32, 32), op=ALU.mult)
            sc.add(heavy, q_w1, reads=["qa", "sc_s"], writes=["qa"])

            def q_xd(e, dst=out_ap):
                return e.tensor_tensor(out=dst, in0=qa_s[:], in1=m2ap,
                                       op=ALU.mult)
            sc.add(heavy, q_xd, reads=["qa"] + m2t, writes=[out_tok])

        def emit_wd_head(mc):
            sched_ld("s", x2_s[:, 0:D], wffn_d[mc * P:(mc + 1) * P, :],
                     "ld", ld_n, writes=["x2_0"])
            emit_quantize_head("x2_0", x2_s[:, 0:D], heavy="g")
            emit_quantize_tail("xd_0", xd_sb[:, 0:D], heavy="g",
                               m1=(pt_sb[:, 0:1024], ["ptr0", "ptr1"]),
                               m2=(pt_sb[:, 1024:2048], ["ptr2", "ptr3"]),
                               m3=(xdt_sb[:], ["xdt"]))

        def emit_wd_tr(mc):
            for g in range(2):
                dst = wdt_s[:].rearrange("p (kc c) -> p kc c", c=D)[
                    :, 4 * g:4 * g + 4, mc * P:(mc + 1) * P]
                sched_tr4(xd_sb[:, g * 512:(g + 1) * 512], "xd_0",
                          dst, "wdt", g)

        sched_dump("dbg_wq", wqkv_s[:], ["wqkv_qk", "wqkv_vo"])
        sched_dump("dbg_wdt", wdt_s[:], ["wdt"])

        savedB = None
        for b in range(B_CORE):
            def a_ln(t, b=b):
                xp = t % 2
                xtok = f"x_{xp}"
                sched_ld("s", x_in[:, xp * D:(xp + 1) * D],
                         x_d[b, t * P:(t + 1) * P, :], "ld", ld_n,
                         writes=[xtok])
                sp = 32 * (t % 2)
                stm = f"stm_{t % 2}"
                stv = f"stv_{t % 2}"
                xcb = mb2_s if t % 2 == 0 else mb3_s
                xct = "mb2" if t % 2 == 0 else "mb3"

                def a_sum(e, xp=xp, sp=sp):
                    return e.reduce_sum(out=stat_s[:, sp:sp + 1],
                                        in_=x_in[:, xp * D:(xp + 1) * D],
                                        axis=AX.X)
                sc.add("v", a_sum, reads=[xtok], writes=[stm])

                def a_m(e, sp=sp):
                    return e.tensor_scalar(out=stat_s[:, sp + 1:sp + 2],
                                           in0=stat_s[:, sp:sp + 1],
                                           scalar1=1.0 / D, scalar2=None,
                                           op0=ALU.mult)
                sc.add("v", a_m, reads=[stm], writes=[stm])

                def a_xc(e, xp=xp, sp=sp, xcb=xcb):
                    return e.tensor_scalar(out=xcb[:],
                                           in0=x_in[:, xp * D:(xp + 1) * D],
                                           scalar1=stat_s[:, sp + 1:sp + 2],
                                           scalar2=None, op0=ALU.subtract)
                sc.add("v", a_xc, reads=[xtok, stm], writes=[xct])

                def a_sq(e, sp=sp, xcb=xcb):
                    return e.scalar_tensor_tensor(
                        out=mb1_s[:], in0=xcb[:], scalar=1.0, in1=xcb[:],
                        op0=ALU.mult, op1=ALU.mult,
                        accum_out=stat_s[:, sp + 2:sp + 3])
                sc.add("v", a_sq, reads=[xct], writes=["mb1", stv])

                def a_rstd(e, sp=sp):
                    e.activation(out=stat_s[:, sp + 3:sp + 4],
                                 in_=stat_s[:, sp + 2:sp + 3], func=AF.Ln,
                                 scale=1.0 / D, bias=EPS_LN)
                    e.drain()
                    return e.activation(out=stat_s[:, sp + 4:sp + 5],
                                        in_=stat_s[:, sp + 3:sp + 4],
                                        func=AF.Exp, scale=-0.5)
                sc.add("a", a_rstd, reads=[stv, "eps"], writes=[stv])

                def a_diag(e, sp=sp, dgb=(diag_s if t % 2 == 0 else diagb_s)):
                    return e.tensor_scalar(out=dgb[:], in0=ident_s[:],
                                           scalar1=stat_s[:, sp + 4:sp + 5],
                                           scalar2=None, op0=ALU.mult)
                sc.add("v", a_diag, reads=["ident", stv],
                       writes=[f"diag_{t % 2}"])

            def a_pe(t, b=b):
                xp = t % 2
                xcb = mb2_s if t % 2 == 0 else mb3_s
                xct = "mb2" if t % 2 == 0 else "mb3"
                xnt_view = p_sb[:, xp * D:(xp + 1) * D]
                for g in range(2):
                    j = g % 2
                    tok = f"tr{j}"

                    def a_trx(e, g=g, j=j, xcb=xcb,
                              dgb=(diag_s if t % 2 == 0 else diagb_s)):
                        r = None
                        for kc in range(4):
                            r = e.matmul(
                                out=tr_ps2[j][:, kc * P:(kc + 1) * P],
                                lhsT=xcb[:, (g * 4 + kc) * P:
                                          (g * 4 + kc + 1) * P],
                                rhs=dgb[:], start=True, stop=True)
                        return r
                    sc.add("t", a_trx, reads=[xct, f"diag_{t % 2}"],
                           writes=[tok])

                    def a_evx(e, dst=xnt_view, g=g, j=j):
                        return e.activation(out=dst[:, g * 512:(g + 1) * 512],
                                            in_=tr_ps2[j][:], func=AF.Copy)
                    sc.add("a", a_evx, reads=[tok],
                           writes=[f"p{xp * 2 + g}"])

            def a_mm_qk(t, b=b):
                ptoks = [f"p{(t % 2) * 2}", f"p{(t % 2) * 2 + 1}"]
                xnt_view = p_sb[:, (t % 2) * D:(t % 2 + 1) * D]
                for ocg in range(4):
                    def a_qk(e, ocg=ocg, src=xnt_view):
                        r = None
                        for j in range(4):
                            oc = ocg * 4 + j
                            for kc in range(DC):
                                r = e.matmul(
                                    out=s_ps[ocg][:, j * P:(j + 1) * P],
                                    lhsT=wqkv_s[:, kc * 3 * D + oc * P:
                                                kc * 3 * D + oc * P + P],
                                    rhs=src[:, kc * P:(kc + 1) * P],
                                    start=(kc == 0), stop=(kc == DC - 1))
                        return r
                    sc.add("t", a_qk, reads=ptoks + ["wqkv_qk"],
                           writes=[f"s{ocg}"])

                    def a_evqk(e, ocg=ocg, t=t):
                        dstbase = qt_s if ocg < 2 else kt_s
                        oc0 = (ocg * 4) % 8
                        dst = dstbase[:].rearrange("p (a c) -> p a c", c=NTOK)[
                            :, oc0:oc0 + 4, t * P:(t + 1) * P]
                        return e.activation(
                            out=dst,
                            in_=s_ps[ocg][:].rearrange("p (a c) -> p a c", c=P),
                            func=AF.Copy)
                    sc.add("a", a_evqk, reads=[f"s{ocg}"],
                           writes=(["qt"] if ocg < 2 else
                                   ["kt", "kt0", "kt1"]))

            def a_mm_vo(t, b=b):
                ptoks = [f"p{(t % 2) * 2}", f"p{(t % 2) * 2 + 1}"]
                xnt_view = p_sb[:, (t % 2) * D:(t % 2 + 1) * D]

                def a_vo(e, src=xnt_view):
                    r = None
                    for nh in range(2):
                        for kc in range(DC):
                            r = e.matmul(
                                out=of_ps[nh][:],
                                lhsT=src[:, kc * P:(kc + 1) * P],
                                rhs=wqkv_s[:, kc * 3 * D + 2 * D + nh * 512:
                                           kc * 3 * D + 2 * D + nh * 512 + 512],
                                start=(kc == 0), stop=(kc == DC - 1))
                    return r
                sc.add("t", a_vo, reads=ptoks + ["wqkv_vo"],
                       writes=["of0", "of1"])

                def a_evvo(e, t=t):
                    e.activation(out=vo_s[:, t * D:t * D + 512],
                                 in_=of_ps[0][:], func=AF.Copy)
                    e.activation(out=vo_s[:, t * D + 512:(t + 1) * D],
                                 in_=of_ps[1][:], func=AF.Copy)
                    return None
                sc.add("a", a_evvo, reads=["of0", "of1"], writes=["vo"])

            def emit_A_tile(t, bb=b):
                if t + 2 < NT:
                    a_ln(t + 2, bb)
                a_mm_qk(t, bb)
                if t + 1 < NT:
                    a_pe(t + 1, bb)
                a_mm_vo(t, bb)


            if b == 0:
                sched_dump("dbg_qt", qt_s[:], ["qt"])
                sched_dump("dbg_kt", kt_s[:], ["kt"])
                sched_dump("dbg_vo", vo_s[:], ["vo"])

            def stage_S1(i, b=b):
                xp = i % 2
                sp = 32 * (i % 2)
                stx = f"stx_{i % 2}"
                sched_ld("s", x_in[:, xp * D:(xp + 1) * D],
                         x_d[b, i * P:(i + 1) * P, :], "ld", ld_n,
                         writes=[f"x_{xp}"])

                def b_xpb(e, xp=xp):
                    v = x_in[:, xp * D:(xp + 1) * D]
                    return e.tensor_tensor(out=v, in0=v, in1=outb_s[:],
                                           op=ALU.add)
                sc.add("g", b_xpb, reads=[f"x_{xp}", "outb"],
                       writes=[f"x_{xp}"])

                for nb in range(4):
                    def b_s(e, i=i, nb=nb):
                        r = None
                        for kc in range(DC):
                            r = e.matmul(
                                out=s_ps[nb][:],
                                lhsT=qt_s[:, kc * NTOK + i * P:
                                          kc * NTOK + i * P + P],
                                rhs=kt_s[:, kc * NTOK + nb * 512:
                                         kc * NTOK + nb * 512 + 512],
                                start=(kc == 0), stop=(kc == DC - 1))
                        return r
                    sc.add("t", b_s, reads=["qt", "kt"], writes=[f"s{nb}"])

                def b_max(e, sp=sp):
                    for nb in range(4):
                        e.reduce_max(out=stat_s[:, sp + 8 + nb:sp + 9 + nb],
                                     in_=s_ps[nb][:], axis=AX.X, negate=True)
                    e.drain()
                    return e.tensor_reduce(out=stat_s[:, sp + 12:sp + 13],
                                           in_=stat_s[:, sp + 8:sp + 12],
                                           axis=AX.X, op=ALU.min)
                sc.add("v", b_max, reads=["s0", "s1", "s2", "s3"],
                       writes=[stx])

            def stage_S2(i, b=b):
                sp = 32 * (i % 2)
                stx = f"stx_{i % 2}"
                std = f"std_{i % 2}"

                for nb in range(4):
                    def b_exp(e, sp=sp, nb=nb):
                        return e.activation(
                            out=p_sb[:, nb * 512:(nb + 1) * 512],
                            in_=s_ps[nb][:], func=AF.Exp,
                            bias=stat_s[:, sp + 12:sp + 13], scale=1.0,
                            accum_out=stat_s[:, sp + 16 + nb:sp + 17 + nb])
                    sc.add("a", b_exp, reads=[f"s{nb}", stx],
                           writes=[f"p{nb}", f"std{nb}_{i % 2}"])

                def b_dengather(e, sp=sp):
                    return e.tensor_copy(out=stat_s[:, sp + 22:sp + 23],
                                         in_=stat_s[:, sp + 16:sp + 17])
                sc.add("v", b_dengather,
                       reads=[f"std0_{i % 2}", f"std1_{i % 2}",
                              f"std2_{i % 2}", f"std3_{i % 2}"],
                       writes=[std])

                def b_den(e, sp=sp):
                    e.reduce_sum(out=stat_s[:, sp + 20:sp + 21],
                                 in_=stat_s[:, sp + 16:sp + 20], axis=AX.X)
                    e.drain()
                    return e.reciprocal(out=stat_s[:, sp + 21:sp + 22],
                                        in_=stat_s[:, sp + 20:sp + 21])
                sc.add("v", b_den, reads=[std], writes=[std])

            def stage_trp(i, q, b=b):
                sched_tr4(p_sb[:, (q * 4) * P:(q * 4 + 4) * P], f"p{q}",
                          pt_sb[:, q * 512:(q + 1) * 512], f"ptr{q}", q)

            def stage_o(i, q, b=b):
                def b_o(e, q=q):
                    rr = None
                    for j in range(4):
                        kb = q * 4 + j
                        for nh in range(2):
                            rr = e.matmul(
                                out=of_ps[nh][:],
                                lhsT=pt_sb[:, kb * P:(kb + 1) * P],
                                rhs=vo_s[:, kb * D + nh * 512:
                                         kb * D + nh * 512 + 512],
                                start=(kb == 0), stop=(kb == NT - 1))
                    return rr
                sc.add("t", b_o, reads=[f"ptr{q}", "vo"],
                       writes=["of0", "of1"])

            def stage_Oep(i, b=b):
                xp = i % 2
                sp = 32 * (i % 2)
                stm = f"stm_{i % 2}"
                stv = f"stv_{i % 2}"
                std = f"std_{i % 2}"
                x2tok = f"x2_{i % 2}"
                x2v = x2_s[:, (i % 2) * D:(i % 2 + 1) * D]

                def b_oep(e, xp=xp, sp=sp, x2v=x2v):
                    for nh in range(2):
                        e.scalar_tensor_tensor(
                            out=x2v[:, nh * 512:(nh + 1) * 512],
                            in0=of_ps[nh][:],
                            scalar=stat_s[:, sp + 21:sp + 22],
                            in1=x_in[:, xp * D + nh * 512:
                                     xp * D + nh * 512 + 512],
                            op0=ALU.mult, op1=ALU.add)
                    return None
                sc.add("v", b_oep, reads=["of0", "of1", std, f"x_{xp}"],
                       writes=[x2tok])

                def b_nsum(e, x2v=x2v, sp=sp):
                    return e.reduce_sum(out=stat_s[:, sp + 0:sp + 1], in_=x2v,
                                        axis=AX.X)
                sc.add("v", b_nsum, reads=[x2tok], writes=[stm])

                def b_sq2(e, x2v=x2v, sp=sp):
                    return e.activation(out=mb1_s[:], in_=x2v, func=AF.Square,
                                        accum_out=stat_s[:, sp + 2:sp + 3])
                sc.add("a", b_sq2, reads=[x2tok], writes=["mb1", stv])

                def b_negm(e, sp=sp):
                    return e.tensor_scalar(out=stat_s[:, sp + 1:sp + 2],
                                           in0=stat_s[:, sp + 0:sp + 1],
                                           scalar1=1.0 / D, scalar2=None,
                                           op0=ALU.mult)
                sc.add("v", b_negm, reads=[stm], writes=[stm])

                def b_xc2(e, x2v=x2v, sp=sp):
                    return e.tensor_scalar(out=qa_s[:], in0=x2v,
                                           scalar1=stat_s[:, sp + 1:sp + 2],
                                           scalar2=None, op0=ALU.subtract)
                sc.add("v", b_xc2, reads=[x2tok, stm], writes=["qa"])

                def b_var(e, sp=sp):
                    e.tensor_scalar(out=stat_s[:, sp + 6:sp + 7],
                                    in0=stat_s[:, sp + 1:sp + 2],
                                    scalar1=stat_s[:, sp + 1:sp + 2],
                                    scalar2=None, op0=ALU.mult)
                    e.drain()
                    return e.tensor_scalar(out=stat_s[:, sp + 5:sp + 6],
                                           in0=stat_s[:, sp + 2:sp + 3],
                                           scalar1=1.0 / D,
                                           scalar2=stat_s[:, sp + 6:sp + 7],
                                           op0=ALU.mult, op1=ALU.subtract)
                sc.add("v", b_var, reads=[stv, stm], writes=[stv])

                def b_rstd(e, sp=sp):
                    e.activation(out=stat_s[:, sp + 3:sp + 4],
                                 in_=stat_s[:, sp + 5:sp + 6], func=AF.Ln,
                                 bias=EPS_LN)
                    e.drain()
                    return e.activation(out=stat_s[:, sp + 4:sp + 5],
                                        in_=stat_s[:, sp + 3:sp + 4],
                                        func=AF.Exp, scale=-0.5)
                sc.add("a", b_rstd, reads=[stv, "eps"], writes=[stv])

                emit_quantize_head("qa", qa_s[:],
                                   rstd_ap=stat_s[:, sp + 4:sp + 5],
                                   rstd_tok=stv)

            def stage_QT(i, b=b):
                emit_quantize_tail(f"xd_{i % 2}",
                                   xd_sb[:, (i % 2) * D:(i % 2 + 1) * D])

            def stage_trxd(i, b=b):
                xdv = xd_sb[:, (i % 2) * D:(i % 2 + 1) * D]
                for g in range(2):
                    sched_tr4(xdv[:, g * 512:(g + 1) * 512], f"xd_{i % 2}",
                              xdt_sb[:, g * 512:(g + 1) * 512], "xdt", g)

            def stage_F(i, half, b=b):
                x2tok = f"x2_{i % 2}"
                x2v = x2_s[:, (i % 2) * D:(i % 2 + 1) * D]

                def b_f(e, half=half):
                    for kc in range(DC):
                        e.matmul(
                            out=of_ps[half][:],
                            lhsT=xdt_sb[:, kc * P:(kc + 1) * P],
                            rhs=wdt_s[:, kc * D + half * 512:
                                      kc * D + half * 512 + 512],
                            start=(kc == 0), stop=False)
                    return e.matmul(
                        out=of_ps[half][:], lhsT=ones_r[:],
                        rhs=ffnb_s[0:1, half * 512:(half + 1) * 512],
                        start=False, stop=True)
                sc.add("t", b_f, reads=["xdt", "wdt", "ffnb", "ones"],
                       writes=[f"of{half}"])

                def b_fep(e, x2v=x2v, half=half):
                    return e.tensor_tensor(
                        out=x2v[:, half * 512:(half + 1) * 512],
                        in0=x2v[:, half * 512:(half + 1) * 512],
                        in1=of_ps[half][:], op=ALU.add)
                sc.add("v", b_fep, reads=[f"of{half}", x2tok], writes=[x2tok])

            def stage_sto(i, b=b):
                x2tok = f"x2_{i % 2}"
                x2v = x2_s[:, (i % 2) * D:(i % 2 + 1) * D]
                sto_n[0] += 16
                tgt = sto_n[0]

                def b_sto(e, b=b, i=i, x2v=x2v):
                    e.dma_start(out=out_d[b, i * P:(i + 1) * P, :],
                                in_=x2v).then_inc(sem["sto"], 16)
                    return None
                sc.add("s", b_sto, reads=[x2tok], dma=("sto", tgt))

            def emit_B_iter(i, _f=(stage_S1, stage_trxd,
                                   stage_trp, stage_F, stage_sto, stage_QT,
                                   stage_S2, stage_o, stage_Oep)):
                (f_S1, f_trxd, f_trp, f_F, f_sto, f_QT, f_S2, f_o,
                 f_Oep) = _f
                if i < NT:
                    f_S1(i)
                if i - 3 >= 0:
                    f_trxd(i - 3)
                if 0 <= i - 1 < NT:
                    f_trp(i - 1, 0)
                    f_trp(i - 1, 1)
                if i - 3 >= 0:
                    f_F(i - 3, 0)
                if 0 <= i - 1 < NT:
                    f_trp(i - 1, 2)
                    f_trp(i - 1, 3)
                if i - 3 >= 0:
                    f_F(i - 3, 1)
                    f_sto(i - 3)
                if 0 <= i - 2 < NT:
                    f_QT(i - 2)
                if i < NT:
                    f_S2(i)
                if 0 <= i - 1 < NT:
                    for q in range(4):
                        f_o(i - 1, q)
                if 0 <= i - 1 < NT:
                    f_Oep(i - 1)

            if b > 0:
                prev_iter = savedB[0]
                for i in range(NT, NT + 3):
                    prev_iter(i)

            a_ln(0)
            a_ln(1)
            a_pe(0)
            for t in range(NT):
                emit_A_tile(t)
                if b == 0 and t % 2 == 0:
                    if t >= 2:
                        emit_wd_tr(t // 2 - 1)
                    emit_wd_head(t // 2)
            if b == 0:
                emit_wd_tr(NT // 2 - 1)

            for i in range(NT):
                emit_B_iter(i)
            savedB = (emit_B_iter,)

        for i in range(NT, NT + 3):
            savedB[0](i)

        total_sto = sto_n[0]

        def _fin(e):
            e.wait_ge(sem["sto"], total_sto)
            return None
        sc.add("s", _fin)

        steps = sc.steps
        for s_ in steps:
            for d in s_["deps"]:
                steps[d]["need"] = True
        mcount = {}
        for s_ in steps:
            if s_["dma"] is None and s_["need"]:
                en = s_["eng"]
                mcount[en] = mcount.get(en, 0) + 1
                s_["mno"] = mcount[en]

        with nc.Block() as block:
            def runner(name):
                def run(eng):
                    covered = {}
                    self_cov = 0
                    for k, s_ in enumerate(steps):
                        if s_["eng"] != name:
                            continue
                        waits = {}
                        for d in sorted(s_["deps"]):
                            ds = steps[d]
                            if ds["dma"] is not None:
                                smn, tgt = ds["dma"]
                                if covered.get(smn, 0) < tgt:
                                    waits[smn] = max(waits.get(smn, 0), tgt)
                            elif ds["eng"] == name:
                                assert ds["mno"] is not None, (k, d)
                                if ds["mno"] > self_cov:
                                    waits[name] = max(waits.get(name, 0),
                                                      ds["mno"])
                            else:
                                assert ds["mno"] is not None, (k, d, ds)
                                if covered.get(ds["eng"], 0) < ds["mno"]:
                                    waits[ds["eng"]] = max(
                                        waits.get(ds["eng"], 0), ds["mno"])
                        for smn, v in waits.items():
                            eng.wait_ge(sem[smn], v)
                            if smn == name:
                                self_cov = max(self_cov, v)
                            else:
                                covered[smn] = max(covered.get(smn, 0), v)
                        r = s_["fn"](eng)
                        if s_["dma"] is None and s_["mno"] is not None:
                            if r is None:
                                eng.drain().then_inc(sem[name], 1)
                            else:
                                r.then_inc(sem[name], 1)
                return run
            block.vector(runner("v"))
            block.scalar(runner("a"))
            block.tensor(runner("t"))
            block.gpsimd(runner("g"))
            block.sync(runner("s"))

    return nc


_NC_CACHE = None


def kernel(L0_attn_x, L0_attn_weight_qkv, L0_attn_out_w, L0_attn_out_b,
           L0_ffn_weight, L0_ffn_bias):
    global _NC_CACHE
    x = np.ascontiguousarray(np.asarray(L0_attn_x, dtype=np.float32))
    wqkv = np.ascontiguousarray(np.asarray(L0_attn_weight_qkv, dtype=np.float32))
    out_w = np.asarray(L0_attn_out_w, dtype=np.float32)
    out_b = np.asarray(L0_attn_out_b, dtype=np.float32)
    wffn = np.ascontiguousarray(np.asarray(L0_ffn_weight, dtype=np.float32))
    ffnb = np.asarray(L0_ffn_bias, dtype=np.float32)

    wvt = np.ascontiguousarray(wqkv[:, 2 * D:3 * D].T)
    owt = np.ascontiguousarray(out_w.T)
    outb_bc = np.ascontiguousarray(np.broadcast_to(out_b, (P, D)))
    ffnb_bc = np.ascontiguousarray(np.broadcast_to(ffnb, (P, D)))
    ident = np.eye(P, dtype=ml_dtypes.bfloat16)

    if _NC_CACHE is None:
        _NC_CACHE = build_program()
    nc = _NC_CACHE

    in_maps = []
    for c in range(N_CORES):
        in_maps.append({
            "x": np.ascontiguousarray(x[c * B_CORE:(c + 1) * B_CORE]),
            "wqkv": wqkv,
            "wvt": wvt,
            "owt": owt,
            "outb": outb_bc,
            "wffn": wffn,
            "ffnb": ffnb_bc,
            "ident": ident,
        })
    res = run_bass_kernel_spmd(nc, in_maps, core_ids=list(range(N_CORES)))
    out = np.concatenate([res.results[c]["out"] for c in range(N_CORES)], axis=0)
    return out.astype(np.float32)
